# version 09
# baseline (speedup 1.0000x reference)
import contextlib
import numpy as np
import ml_dtypes

import concourse.bass as bass
import concourse.mybir as mybir
from concourse.bass_utils import run_bass_kernel_spmd

F32 = mybir.dt.float32
BF16 = mybir.dt.bfloat16
AF = mybir.ActivationFunctionType
ALU = mybir.AluOpType
BF = ml_dtypes.bfloat16

N = 8192
D = 1024
H = 16
DH = 64
CG_ITERS = 30
SEG = 512
NTB = SEG // 128


def build(n_tokens=N, cg_iters=CG_ITERS, stage=99):
    import os
    stage = int(os.environ.get("K_STAGE", stage))
    p0 = int(os.environ.get("K_P0", 4))
    cgl = int(os.environ.get("K_CG", 7))
    nseg = n_tokens // SEG
    nc = bass.Bass("TRN2", target_bir_lowering=False, detect_race_conditions=False)

    x_in = nc.dram_tensor("x", [n_tokens, D], F32, kind="ExternalInput").ap()
    wq = nc.dram_tensor("wq", [D, D], F32, kind="ExternalInput").ap()
    wk = nc.dram_tensor("wk", [D, D], F32, kind="ExternalInput").ap()
    wv = nc.dram_tensor("wv", [D, D], F32, kind="ExternalInput").ap()
    wf = nc.dram_tensor("wf", [D, H], F32, kind="ExternalInput").ap()
    wb = nc.dram_tensor("wb", [D, H], F32, kind="ExternalInput").ap()
    wo = nc.dram_tensor("wo", [D, D], F32, kind="ExternalInput").ap()
    delta = nc.dram_tensor("delta", [1, H], F32, kind="ExternalInput").ap()
    lambp = nc.dram_tensor("lambp", [1, D], F32, kind="ExternalInput").ap()
    nw_in = nc.dram_tensor("nw", [1, D], F32, kind="ExternalInput").ap()
    g1 = nc.dram_tensor("g1", [D, DH], F32, kind="ExternalInput").ap()
    g2 = nc.dram_tensor("g2", [DH, D], F32, kind="ExternalInput").ap()
    gw = nc.dram_tensor("gw", [D, 8], F32, kind="ExternalInput").ap()
    gb = nc.dram_tensor("gb", [1, 8], F32, kind="ExternalInput").ap()
    we = nc.dram_tensor("we", [8, D, D], F32, kind="ExternalInput").ap()
    eb = nc.dram_tensor("eb", [8, D], F32, kind="ExternalInput").ap()
    cid16 = nc.dram_tensor("cid16", [128, 128], BF16, kind="ExternalInput").ap()
    cidf = nc.dram_tensor("cidf", [128, 128], F32, kind="ExternalInput").ap()
    ctriu = nc.dram_tensor("ctriu", [128, 128], F32, kind="ExternalInput").ap()
    ctrim = nc.dram_tensor("ctrim", [128, 128], BF16, kind="ExternalInput").ap()

    out_t = nc.dram_tensor("out", [n_tokens, D], F32, kind="ExternalOutput").ap()
    dbg_t = (nc.dram_tensor("dbg", [128, 8192], F32, kind="ExternalOutput").ap()
             if os.environ.get("K_DUMP") else None)

    wq_b = nc.dram_tensor("wq_b", [D, D], BF16, kind="Internal").ap()
    wk_b = nc.dram_tensor("wk_b", [D, D], BF16, kind="Internal").ap()
    wv_b = nc.dram_tensor("wv_b", [D, D], BF16, kind="Internal").ap()
    wo_b = nc.dram_tensor("wo_b", [D, D], BF16, kind="Internal").ap()
    we_b = nc.dram_tensor("we_b", [8, D, D], BF16, kind="Internal").ap()
    lam_d = nc.dram_tensor("lam_d", [1, D], F32, kind="Internal").ap()

    es = contextlib.ExitStack()
    sb = lambda n_, s, d: es.enter_context(nc.sbuf_tensor(n_, s, d))
    psumt = lambda n_, s, d=F32: es.enter_context(nc.psum_tensor(n_, s, d))
    sem = lambda n_: es.enter_context(nc.semaphore(n_))

    x_s = sb("x_s", [128, D], F32)
    xnF = sb("xnF", [128, 8, SEG], BF16)
    xnT = sb("xnT", [128, NTB, D], BF16)
    R_s = sb("R_s", [128, NTB, H, DH], F32)
    X_s = sb("X_s", [128, NTB, H, DH], F32)
    P_s = sb("P_s", [128, NTB, H, DH], BF16)
    Ptf = sb("Ptf", [128, 2, 8, 128], BF16)
    kTM = sb("kTM", [128, NTB, H, DH], BF16)
    vTM = sb("vTM", [128, NTB, H, DH], BF16)
    u8k = sb("u8k", [128, 4096], BF16)
    gate_s = sb("gate_s", [128, NTB, H, DH], BF16)
    M_s = sb("M_s", [128, NTB, H, 128], BF16)
    Sh_s = sb("Sh_s", [128, 2, H, 128], BF16)
    Ap_s = sb("Ap_s", [128, 1, H, DH], F32)
    TlP = sb("TlP", [128, H, DH], BF16)
    lbc = sb("lbc", [128, D], BF16)
    Hm = sb("Hm", [128, 2, 8, DH], F32)
    Hb = sb("Hb", [128, NTB, 8, DH], BF16)
    Hbv = sb("Hbv", [128, NTB, 8, DH], BF16)
    wKt = sb("wKt", [128, H, DH], BF16)
    wr = sb("wr", [128, 12, D], BF16)
    g1_s = sb("g1_s", [128, 8, DH], BF16)
    g2_s = sb("g2_s", [64, D], BF16)
    eb_s = sb("eb_s", [8, D], BF16)
    fbw_s = sb("fbw_s", [128, 8, 40], BF16)
    ident = sb("ident", [128, 128], BF16)
    identf = sb("identf", [128, 128], F32)
    triu = sb("triu", [128, 128], F32)
    trim = sb("trim", [128, 128], BF16)
    onesc = sb("onesc", [1, 128], F32)
    nw_s = sb("nw_s", [128, 8], F32)
    delta_s = sb("delta_s", [128, H], F32)
    gbb = sb("gbb", [128, 8], F32)
    lrow = sb("lrow", [1, D], F32)
    lrow2 = sb("lrow2", [1, D], F32)
    grow = sb("grow", [1, 4096], BF16)
    fln = sb("fln", [128, NTB, H], F32)
    Gc = sb("Gc", [128, NTB, H], F32)
    nGsh = sb("nGsh", [128, NTB, H], F32)
    lnb_t = sb("lnb_t", [128, NTB, H], F32)
    beta_t = sb("beta_t", [128, NTB, H], F32)
    dec_s = sb("dec_s", [128, NTB, H], F32)
    dece = sb("dece", [128, H], F32)
    wpr = sb("wpr", [128, H], F32)
    GFs = sb("GFs", [32, 128], BF16)
    fhl = sb("fhl", [128, NTB, 32], BF16)
    ghl = sb("ghl", [128, 32], BF16)
    onesall = sb("onesall", [128, 128], BF16)
    onesb = sb("onesb", [1, 128], BF16)
    rsA = sb("rsA", [128, NTB, H], F32)
    rsB = sb("rsB", [128, NTB, H], F32)
    pap_t = sb("pap_t", [128, NTB, H], F32)
    alp_t = sb("alp_t", [128, H], F32)
    nalp_t = sb("nalp_t", [128, H], F32)
    bet_t = sb("bet_t", [128, H], F32)
    t16a = sb("t16a", [128, H], F32)
    t16b = sb("t16b", [128, H], F32)
    t16c = sb("t16c", [128, H], F32)
    t16d = sb("t16d", [128, H], F32)
    junkb = sb("junkb", [128, DH], BF16)
    junkc = sb("junkc", [128, 256], F32)
    scr2b = sb("scr2b", [128, D], BF16)
    dec_sh = sb("dec_sh", [128, NTB, H], F32)
    probs = sb("probs", [128, NTB, 8], F32)
    wfin = sb("wfin", [128, NTB, 8], F32)
    wf8 = sb("wf8", [128, 8], BF16)
    wfT = sb("wfT", [8, 128], BF16)
    t1F = sb("t1F", [64, SEG], BF16)
    scr = sb("scr", [128, 1024], F32)
    scr2 = sb("scr2", [128, 1024], F32)
    qss = sb("qss", [128, H], F32)

    pS = psumt("pS", [128, H, 128], F32)
    pPt = psumt("pPt", [128, 2, 8, 128], BF16)
    pAp = psumt("pAp", [128, H, DH], F32)
    pS_f = pS[:].rearrange("p h t -> p (h t)")
    pAp_f = pAp[:].rearrange("p h j -> p (h j)")

    dmaS = sem("dmaS"); dmaW = sem("dmaW")
    sT = sem("sT"); sV = sem("sV"); sSc = sem("sSc"); sTW = sem("sTW")
    cT1 = sem("cT1"); cT2 = sem("cT2"); cT3 = sem("cT3")
    cPt = sem("cPt"); cSh = sem("cSh"); cAp = sem("cAp"); cP = sem("cP")

    _TRACKED = {"matmul", "transpose", "tensor_copy", "tensor_tensor",
                "tensor_scalar", "scalar_tensor_tensor", "tensor_single_scalar",
                "activation", "memset", "memzero", "reciprocal", "reduce_max",
                "reduce_sum", "tensor_reduce", "copy", "add", "mul", "nop"}
    _last = {}

    class _EP:
        def __init__(self, eng, key):
            self._e = eng; self._k = key
        def __getattr__(self, n):
            a = getattr(self._e, n)
            if n in _TRACKED:
                def f(*args, **kw):
                    r = a(*args, **kw)
                    _last[self._k] = r
                    return r
                return f
            return a

    HORD = list(range(0, H, 2)) + list(range(1, H, 2))
    TEN, VEC, SCA = _EP(nc.tensor, "T"), _EP(nc.vector, "V"), _EP(nc.scalar, "S")
    SYN, GPS = nc.sync, nc.gpsimd
    rT1 = TEN.alloc_register(); rT3 = TEN.alloc_register()
    rT0 = TEN.alloc_register(); rTs = TEN.alloc_register()
    rV1 = VEC.alloc_register(); rVs = VEC.alloc_register()
    rS1 = SCA.alloc_register()
    _BARR = [mybir.EngineType.PE, mybir.EngineType.DVE,
             mybir.EngineType.Activation, mybir.EngineType.SP]

    def barrier():
        nc.multi_engine_barrier(_BARR)
    nS = [0]; nW = [0]; nT = [0]; nV = [0]; nSc = [0]; nTW = [0]

    def dmas(out, in_, slow=False):
        SYN.dma_start(out=out, in_=in_, allow_slow_non_contiguous=slow).then_inc(dmaS, 16)
        nS[0] += 1

    def dmaw(out, in_):
        GPS.dma_start(out=out, in_=in_).then_inc(dmaW, 16)
        nW[0] += 1

    def tinc(instr=None):
        (_last["T"] if instr is None else instr).then_inc(sT, 1); nT[0] += 1

    def vinc(instr=None):
        (_last["V"] if instr is None else instr).then_inc(sV, 1); nV[0] += 1

    def scinc(instr=None):
        (_last["S"] if instr is None else instr).then_inc(sSc, 1); nSc[0] += 1

    def spacer():
        VEC.tensor_scalar(out=junkc[:], in0=junkc[:], scalar1=1.0, scalar2=None, op0=ALU.mult)

    def tw_done():
        VEC.memset(junkb[0:1, 0:1], 0.0).then_inc(sTW, 1); nTW[0] += 1

    dmas(identf[:], cidf)
    dmas(triu[:], ctriu)
    dmas(ident[:], cid16)
    dmas(trim[:], ctrim)
    dmas(nw_s[:].rearrange("p b -> p b"), nw_in.rearrange("one (b p) -> p one b", p=128).rearrange("p one b -> p (one b)"), slow=True)
    dmas(lrow[:], lambp)
    dmas(lrow2[0:1, 0:H], delta)
    dmas(lrow2[0:1, H:H + 8], gb)
    VEC.wait_ge(dmaS, 16 * nS[0])
    VEC.memset(onesc[:], 1.0)
    VEC.memset(onesb[:], 1.0)
    VEC.memset(onesall[:], 1.0)
    VEC.memset(Hm[:].rearrange("p a b j -> p (a b j)"), 0.0)
    vinc(VEC.memset(junkb[:], 0.0))
    dmas(delta_s[:].unsqueeze(1), delta.partition_broadcast(128), slow=True)
    dmas(gbb[:].unsqueeze(1), gb.partition_broadcast(128), slow=True)

    if p0 < 1:
        barrier(); SYN.wait_ge(dmaS, 16 * nS[0]); barrier(); es.close(); return nc
    SCA.wait_ge(sV, nV[0])
    SCA.activation(out=scr[0:1, 0:D], in_=lrow[:], func=AF.Abs)
    SCA.drain()
    scinc(SCA.activation(out=scr[0:1, 0:D], in_=scr[0:1, 0:D], func=AF.Exp, scale=-1.0))
    VEC.wait_ge(sSc, nSc[0])
    VEC.tensor_scalar(out=scr[0:1, 0:D], in0=scr[0:1, 0:D], scalar1=1.0, scalar2=None, op0=ALU.add)
    vinc()
    SCA.wait_ge(sV, nV[0])
    scinc(SCA.activation(out=scr[0:1, 0:D], in_=scr[0:1, 0:D], func=AF.Ln))
    VEC.wait_ge(sSc, nSc[0])
    VEC.tensor_single_scalar(out=scr2[0:1, 0:D], in_=lrow[:], scalar=0.0, op=ALU.max)
    VEC.drain()
    VEC.tensor_tensor(out=scr[0:1, 0:D], in0=scr[0:1, 0:D], in1=scr2[0:1, 0:D], op=ALU.add)
    VEC.drain()
    VEC.tensor_scalar(out=lrow[0:1, :], in0=scr[0:1, 0:D], scalar1=0.25, scalar2=None, op0=ALU.add)
    vinc()
    if p0 < 2:
        barrier(); SYN.wait_ge(dmaS, 16 * nS[0]); barrier(); es.close(); return nc
    SYN.wait_ge(sV, nV[0])
    dmas(lam_d, lrow[:])
    dmas(scr[:].unsqueeze(1), lam_d.partition_broadcast(128), slow=True)
    VEC.wait_ge(dmaS, 16 * nS[0])
    vinc(VEC.tensor_copy(lbc[:], scr[:, 0:D]))

    if p0 < 3:
        barrier(); SYN.wait_ge(dmaS, 16 * nS[0]); barrier(); es.close(); return nc
    cast_jobs = []
    for db in range(8):
        cast_jobs.append((wq[db * 128:(db + 1) * 128, :], wq_b[db * 128:(db + 1) * 128, :], None))
        cast_jobs.append((wk[db * 128:(db + 1) * 128, :], wk_b[db * 128:(db + 1) * 128, :], None))
        cast_jobs.append((wv[db * 128:(db + 1) * 128, :], wv_b[db * 128:(db + 1) * 128, :], None))
        cast_jobs.append((wo[db * 128:(db + 1) * 128, :], wo_b[db * 128:(db + 1) * 128, :], db))
        for e in range(8):
            cast_jobs.append((we[e, db * 128:(db + 1) * 128, :], we_b[e, db * 128:(db + 1) * 128, :], None))
    for j, (src, dst, nwdb) in enumerate(cast_jobs):
        buf = scr[:, 0:D] if j % 2 == 0 else scr2[:, 0:D]
        stg = wr[:, j % 2, :]
        SYN.wait_ge(sV, nV[0])
        dmas(buf, src)
        VEC.wait_ge(dmaS, 16 * nS[0])
        if nwdb is None:
            vinc(VEC.tensor_copy(stg, buf))
        else:
            vinc(VEC.tensor_scalar(out=stg, in0=buf, scalar1=nw_s[:, nwdb:nwdb + 1],
                                   scalar2=None, op0=ALU.mult))
        SYN.wait_ge(sV, nV[0])
        dmas(dst, stg)
    if p0 < 4:
        barrier(); SYN.wait_ge(dmaS, 16 * nS[0]); barrier(); es.close(); return nc
    SYN.wait_ge(sV, nV[0])
    dmas(scr[:, 0:512].rearrange("p (b j) -> p b j", b=8), g1.rearrange("(b p) j -> p b j", p=128), slow=True)
    VEC.wait_ge(dmaS, 16 * nS[0])
    vinc(VEC.tensor_copy(g1_s[:].rearrange("p b j -> p (b j)"), scr[:, 0:512]))
    SYN.wait_ge(sV, nV[0])
    dmas(scr[0:64, 0:D], g2)
    VEC.wait_ge(dmaS, 16 * nS[0])
    vinc(VEC.tensor_copy(g2_s[:], scr[0:64, 0:D]))
    SYN.wait_ge(sV, nV[0])
    dmas(scr[0:8, 0:D], eb)
    VEC.wait_ge(dmaS, 16 * nS[0])
    vinc(VEC.tensor_copy(eb_s[:], scr[0:8, 0:D]))
    for db in range(8):
        SYN.wait_ge(sV, nV[0])
        dmas(scr2[:, 0:16], wf[db * 128:(db + 1) * 128, :])
        dmas(scr2[:, 16:32], wb[db * 128:(db + 1) * 128, :])
        dmas(scr2[:, 32:40], gw[db * 128:(db + 1) * 128, :])
        VEC.wait_ge(dmaS, 16 * nS[0])
        vinc(VEC.tensor_copy(fbw_s[:, db, :], scr2[:, 0:40]))

    barrier()

    HALVES_PER_SEG = 24

    def half_slots(k):
        return [(k * 4 + d4) % 12 for d4 in range(4)]

    for seg in range(nseg):
        t0 = seg * SEG
        half_base = seg * HALVES_PER_SEG
        ring_base = nTW[0]

        def wait_half(hk):
            TEN.wait_ge(dmaW, 16 * 4 * (half_base + hk + 1))

        if seg == 0 and stage >= 2:
            GPS.wait_ge(dmaS, 16 * nS[0])
        ring_seq = [(wq_b, 0), (wq_b, 1), (wk_b, 0), (wk_b, 1), (wv_b, 0), (wv_b, 1),
                    (wo_b, 0), (wo_b, 1)]
        for e in range(8):
            ring_seq.append((we_b[e], 0)); ring_seq.append((we_b[e], 1))
        n_halves = 0 if stage < 2 else (6 if stage < 7 else (8 if stage < 8 else 24))
        for k, (src, hf) in enumerate(ring_seq[:n_halves]):
            if k >= 3:
                GPS.wait_ge(sTW, ring_base + (k - 3) // 2 + 1)
            elif seg > 0:
                GPS.wait_ge(sTW, ring_base - 12 + (21 + k) // 2 + 1)
            for d4 in range(4):
                db = hf * 4 + d4
                dmaw(wr[:, half_slots(k)[d4], :], src[db * 128:(db + 1) * 128, :])

        if stage < 1:
            continue
        for tb in range(NTB):
            SYN.wait_ge(sV, nV[0])
            dmas(x_s[:], x_in[t0 + tb * 128: t0 + (tb + 1) * 128, :])
            VEC.wait_ge(dmaS, 16 * nS[0])
            xv = x_s[:]
            VEC.tensor_reduce(out=qss[:, 0:1], in_=xv, axis=mybir.AxisListType.X, op=ALU.add)
            VEC.drain()
            VEC.tensor_scalar(out=qss[:, 1:2], in0=qss[:, 0:1], scalar1=1.0 / D, scalar2=None, op0=ALU.mult)
            VEC.drain()
            VEC.tensor_tensor(out=scr[:, 0:D], in0=xv,
                              in1=qss[:, 1:2].broadcast_to([128, D]), op=ALU.subtract)
            VEC.scalar_tensor_tensor(out=scr2[:, 0:D], in0=scr[:, 0:D], scalar=1.0,
                                     in1=scr[:, 0:D], op0=ALU.mult, op1=ALU.mult,
                                     accum_out=qss[:, 2:3])
            VEC.drain()
            VEC.tensor_scalar(out=qss[:, 3:4], in0=qss[:, 2:3], scalar1=1.0 / D, scalar2=1e-5, op0=ALU.mult, op1=ALU.add)
            vinc()
            SCA.wait_ge(sV, nV[0])
            scinc(SCA.activation(out=qss[:, 4:5], in_=qss[:, 3:4], func=AF.Sqrt))
            VEC.wait_ge(sSc, nSc[0])
            VEC.reciprocal(out=qss[:, 5:6], in_=qss[:, 4:5])
            VEC.drain()
            vinc(VEC.tensor_tensor(out=xnT[:, tb, :], in0=scr[:, 0:D],
                                   in1=qss[:, 5:6].broadcast_to([128, D]), op=ALU.mult))

        nT_base = nT[0]; nV_base = nV[0]
        for tb in range(NTB):
            for db in range(8):
                idx = tb * 8 + db
                TEN.wait_ge(sV, nV_base if idx < 2 else nV_base + idx - 1)
                tinc(TEN.transpose(pPt[:, idx % 2, 0, :], xnT[:, tb, db * 128:(db + 1) * 128],
                                   ident[:]))
        for tb in range(NTB):
            for db in range(8):
                idx = tb * 8 + db
                VEC.wait_ge(sT, nT_base + idx + 1)
                vinc(VEC.tensor_copy(xnF[:, db, tb * 128:(tb + 1) * 128], pPt[:, idx % 2, 0, :]))

        if stage < 2:
            continue
        def proj_group(gidx, evac):
            for ch in range(2):
                for tb in range(NTB):
                    bank = pS[:, tb * 4:(tb + 1) * 4, :].rearrange("p a t -> p (a t)")
                    TEN.wait_ge(sV, nV[0])
                    for kh in range(2):
                        hk = 2 * gidx + kh
                        wait_half(hk)
                        for d4 in range(4):
                            db = kh * 4 + d4
                            TEN.matmul(bank[:, 0:512], xnF[:, db, tb * 128:(tb + 1) * 128],
                                       wr[:, half_slots(hk)[d4], ch * 512:(ch + 1) * 512],
                                       start=(db == 0), stop=(db == 7))
                    tinc()
                    evac(tb, ch, bank[:, 0:512])
            tw_done()

        def evac_q(tb, ch, ps):
            VEC.wait_ge(sT, nT[0])
            vinc(VEC.tensor_copy(R_s[:, tb, :, :].rearrange("p h j -> p (h j)")[:, ch * 512:(ch + 1) * 512], ps))

        def evac_v(tb, ch, ps):
            VEC.wait_ge(sT, nT[0])
            vinc(VEC.tensor_copy(vTM[:, tb, :, :].rearrange("p h j -> p (h j)")[:, ch * 512:(ch + 1) * 512], ps))

        proj_group(0, evac_q)
        for tb in range(NTB):
            VEC.tensor_tensor(out=scr2[:, 0:D].rearrange("p (h j) -> p h j", h=H),
                              in0=R_s[:, tb, :, :], in1=R_s[:, tb, :, :], op=ALU.mult)
            VEC.drain()
            VEC.tensor_reduce(out=qss[:].unsqueeze(2),
                              in_=scr2[:, 0:D].rearrange("p (h j) -> p h j", h=H),
                              axis=mybir.AxisListType.X, op=ALU.add)
            VEC.drain()
            VEC.tensor_scalar(out=qss[:], in0=qss[:], scalar1=1e-6, scalar2=None, op0=ALU.add)
            vinc()
            SCA.wait_ge(sV, nV[0])
            scinc(SCA.activation(out=t16a[:], in_=qss[:], func=AF.Sqrt))
            VEC.wait_ge(sSc, nSc[0])
            VEC.reciprocal(out=t16b[:], in_=t16a[:])
            VEC.drain()
            VEC.tensor_tensor(out=R_s[:, tb, :, :], in0=R_s[:, tb, :, :],
                              in1=t16b[:].unsqueeze(2).broadcast_to([128, H, DH]), op=ALU.mult)
            VEC.tensor_copy(P_s[:, tb, :, :].rearrange("p h j -> p (h j)"),
                            R_s[:, tb, :, :].rearrange("p h j -> p (h j)"))
            VEC.drain()
            VEC.tensor_tensor(out=scr2[:, 0:D].rearrange("p (h j) -> p h j", h=H),
                              in0=R_s[:, tb, :, :], in1=R_s[:, tb, :, :], op=ALU.mult)
            VEC.drain()
            VEC.tensor_reduce(out=rsA[:, tb, :].unsqueeze(2),
                              in_=scr2[:, 0:D].rearrange("p (h j) -> p h j", h=H),
                              axis=mybir.AxisListType.X, op=ALU.add)
            vinc(VEC.memset(X_s[:, tb, :, :].rearrange("p h j -> p (h j)"), 0.0))

        for tb in range(NTB):
            bank = pS[:, tb * 4:(tb + 1) * 4, :].rearrange("p a t -> p (a t)")
            for ch in range(2):
                TEN.wait_ge(sV, nV[0])
                for kh in range(2):
                    hk = 2 + kh
                    wait_half(hk)
                    for d4 in range(4):
                        db = kh * 4 + d4
                        TEN.matmul(bank[:, 0:512], xnF[:, db, tb * 128:(tb + 1) * 128],
                                   wr[:, half_slots(hk)[d4], ch * 512:(ch + 1) * 512],
                                   start=(db == 0), stop=(db == 7))
                tinc()
                VEC.wait_ge(sT, nT[0])
                vinc(VEC.tensor_copy(scr[:, ch * 512:(ch + 1) * 512], bank[:, 0:512]))
            VEC.tensor_tensor(out=scr2[:, 0:D].rearrange("p (h j) -> p h j", h=H),
                              in0=scr[:, 0:D].rearrange("p (h j) -> p h j", h=H),
                              in1=scr[:, 0:D].rearrange("p (h j) -> p h j", h=H), op=ALU.mult)
            VEC.drain()
            VEC.tensor_reduce(out=qss[:].unsqueeze(2),
                              in_=scr2[:, 0:D].rearrange("p (h j) -> p h j", h=H),
                              axis=mybir.AxisListType.X, op=ALU.add)
            VEC.drain()
            VEC.tensor_scalar(out=qss[:], in0=qss[:], scalar1=1e-6, scalar2=None, op0=ALU.add)
            vinc()
            SCA.wait_ge(sV, nV[0])
            scinc(SCA.activation(out=t16a[:], in_=qss[:], func=AF.Sqrt))
            VEC.wait_ge(sSc, nSc[0])
            VEC.reciprocal(out=t16b[:], in_=t16a[:])
            VEC.drain()
            VEC.tensor_tensor(out=kTM[:, tb, :, :],
                              in0=scr[:, 0:D].rearrange("p (h j) -> p h j", h=H),
                              in1=t16b[:].unsqueeze(2).broadcast_to([128, H, DH]), op=ALU.mult)
            vinc()
        tw_done()

        proj_group(2, evac_v)

        if stage < 3:
            continue
        for tb in range(NTB):
            TEN.wait_ge(sV, nV[0]); TEN.wait_ge(sSc, nSc[0])
            for db in range(8):
                TEN.matmul(pAp_f[:, 0:40], xnF[:, db, tb * 128:(tb + 1) * 128],
                           fbw_s[:, db, :], start=(db == 0), stop=(db == 7))
            tinc()
            VEC.wait_ge(sT, nT[0])
            VEC.tensor_tensor(out=scr[:, 0:H], in0=pAp_f[:, 0:H], in1=delta_s[:], op=ALU.add)
            vinc()
            SCA.wait_ge(sV, nV[0])
            SCA.activation(out=scr[:, H:2 * H].rearrange("p a -> p a"), in_=scr[:, 0:H], func=AF.Abs)
            SCA.drain()
            scinc(SCA.activation(out=scr[:, H:2 * H], in_=scr[:, H:2 * H], func=AF.Exp, scale=-1.0))
            VEC.wait_ge(sSc, nSc[0])
            VEC.tensor_scalar(out=scr[:, H:2 * H], in0=scr[:, H:2 * H], scalar1=1.0, scalar2=None, op0=ALU.add)
            vinc()
            SCA.wait_ge(sV, nV[0])
            scinc(SCA.activation(out=scr[:, H:2 * H], in_=scr[:, H:2 * H], func=AF.Ln))
            scinc(SCA.activation(out=beta_t[:, tb, :], in_=pAp_f[:, H:2 * H], func=AF.Sigmoid))
            VEC.wait_ge(sSc, nSc[0])
            VEC.tensor_single_scalar(out=scr[:, 2 * H:3 * H], in_=scr[:, 0:H], scalar=0.0, op=ALU.min)
            VEC.drain()
            VEC.tensor_tensor(out=fln[:, tb, :], in0=scr[:, 2 * H:3 * H], in1=scr[:, H:2 * H], op=ALU.subtract)
            VEC.drain()
            VEC.tensor_copy(fhl[:, tb, 0:16], fln[:, tb, :])
            VEC.drain()
            VEC.tensor_tensor(out=fhl[:, tb, 16:32], in0=fln[:, tb, :], in1=fhl[:, tb, 0:16], op=ALU.subtract)
            vinc()
            SCA.wait_ge(sV, nV[0])
            scinc(SCA.activation(out=lnb_t[:, tb, :], in_=beta_t[:, tb, :], func=AF.Ln))
            VEC.wait_ge(sSc, nSc[0])
            VEC.tensor_tensor(out=scr[:, 32:40], in0=pAp_f[:, 2 * H:2 * H + 8], in1=gbb[:], op=ALU.add)
            VEC.drain()
            VEC.reduce_max(out=qss[:, 0:1], in_=scr[:, 32:40], axis=mybir.AxisListType.X)
            VEC.drain()
            VEC.tensor_tensor(out=scr[:, 0:8], in0=scr[:, 32:40],
                              in1=qss[:, 0:1].broadcast_to([128, 8]), op=ALU.subtract)
            vinc()
            SCA.wait_ge(sV, nV[0])
            scinc(SCA.activation(out=scr[:, 8:16], in_=scr[:, 0:8], func=AF.Exp))
            VEC.wait_ge(sSc, nSc[0])
            VEC.tensor_reduce(out=qss[:, 1:2], in_=scr[:, 8:16], axis=mybir.AxisListType.X, op=ALU.add)
            VEC.drain()
            VEC.reciprocal(out=qss[:, 2:3], in_=qss[:, 1:2])
            VEC.drain()
            VEC.tensor_tensor(out=probs[:, tb, :], in0=scr[:, 8:16],
                              in1=qss[:, 2:3].broadcast_to([128, 8]), op=ALU.mult)
            VEC.drain()
            VEC.tensor_single_scalar(out=scr[:, 16:24], in_=probs[:, tb, :], scalar=0.1, op=ALU.is_ge)
            VEC.drain()
            VEC.tensor_tensor(out=scr[:, 24:32], in0=probs[:, tb, :], in1=scr[:, 16:24], op=ALU.mult)
            VEC.drain()
            VEC.tensor_reduce(out=qss[:, 3:4], in_=scr[:, 24:32], axis=mybir.AxisListType.X, op=ALU.add)
            VEC.drain()
            VEC.tensor_single_scalar(out=qss[:, 4:5], in_=qss[:, 3:4], scalar=0.0, op=ALU.is_equal)
            VEC.drain()
            VEC.tensor_tensor(out=qss[:, 5:6], in0=qss[:, 3:4], in1=qss[:, 4:5], op=ALU.add)
            VEC.drain()
            VEC.reciprocal(out=qss[:, 6:7], in_=qss[:, 5:6])
            VEC.drain()
            vinc(VEC.tensor_tensor(out=wfin[:, tb, :], in0=scr[:, 24:32],
                                   in1=qss[:, 6:7].broadcast_to([128, 8]), op=ALU.mult))

        TEN.wait_ge(sV, nV[0])
        for db in range(8):
            TEN.matmul(pAp_f[0:64, 0:512], g1_s[:, db, :], xnF[:, db, :],
                       start=(db == 0), stop=(db == 7))
        tinc()
        VEC.wait_ge(sT, nT[0])
        vinc(VEC.tensor_copy(t1F[:], pAp_f[0:64, 0:512]))
        for tb in range(NTB):
            TEN.wait_ge(sV, nV[0]); TEN.wait_ge(sSc, nSc[0])
            for chh in range(2):
                TEN.matmul(pS_f[:, chh * 512:(chh + 1) * 512], t1F[:, tb * 128:(tb + 1) * 128],
                           g2_s[:, chh * 512:(chh + 1) * 512], start=True, stop=True)
            tinc()
            SCA.wait_ge(sT, nT[0])
            scinc(SCA.activation(out=gate_s[:, tb, :, :].rearrange("p h j -> p (h j)"),
                                 in_=pS_f[:, 0:1024], func=AF.Sigmoid))

        if stage < 4:
            continue
        kF = u8k[:].rearrange("p (c hp t) -> p c hp t", c=NTB, hp=8)
        for cch in range(NTB):
            TEN.wait_ge(sV, nV[0]); TEN.wait_ge(sSc, nSc[0])
            TEN.matmul(pAp_f[:, 0:32], trim[:], fhl[:, cch, :], start=True, stop=True)
            TEN.matmul(pAp_f[:, 32:64], onesall[:], fhl[:, cch, :], start=True, stop=True)
            tinc()
            VEC.wait_ge(sT, nT[0])
            VEC.tensor_copy(scr[:, 0:64], pAp_f[:, 0:64])
            VEC.drain()
            VEC.tensor_tensor(out=Gc[:, cch, :], in0=scr[:, 0:16], in1=scr[:, 16:32], op=ALU.add)
            VEC.tensor_tensor(out=t16a[:], in0=scr[:, 32:48], in1=scr[:, 48:64], op=ALU.add)
            VEC.drain()
            VEC.tensor_tensor(out=nGsh[:, cch, :], in0=lnb_t[:, cch, :], in1=Gc[:, cch, :], op=ALU.subtract)
            VEC.drain()
            VEC.tensor_tensor(out=t16b[:], in0=t16a[:], in1=nGsh[:, cch, :], op=ALU.add)
            VEC.tensor_copy(ghl[:, 0:16], Gc[:, cch, :])
            VEC.drain()
            VEC.tensor_tensor(out=ghl[:, 16:32], in0=Gc[:, cch, :], in1=ghl[:, 0:16], op=ALU.subtract)
            vinc()
            SCA.wait_ge(sV, nV[0])
            scinc(SCA.activation(out=dec_s[:, cch, :], in_=Gc[:, cch, :], func=AF.Exp))
            scinc(SCA.activation(out=dece[:], in_=t16a[:], func=AF.Exp))
            scinc(SCA.activation(out=wpr[:], in_=t16b[:], func=AF.Exp))
            TEN.wait_ge(sV, nV[0])
            TEN.transpose(pPt[0:32, cch % 2, 0, :], ghl[:], ident[:])
            tinc()
            VEC.wait_ge(sT, nT[0])
            vinc(VEC.tensor_copy(GFs[:], pPt[0:32, cch % 2, 0, :]))
            SYN.wait_ge(sV, nV[0])
            for hh in range(32):
                dmas(grow[0:1, hh * 128:(hh + 1) * 128], GFs[hh:hh + 1, :])
            TEN.wait_ge(dmaS, 16 * nS[0])
            VEC.wait_ge(sSc, nSc[0])
            dsv = dec_s[:, cch, :].rearrange("p (a b) -> p a b", b=2)
            VEC.tensor_copy(dec_sh[:, cch, 0:8], dsv[:, :, 0:1].rearrange("p a b -> p (a b)"))
            VEC.tensor_copy(dec_sh[:, cch, 8:16], dsv[:, :, 1:2].rearrange("p a b -> p (a b)"))
            vinc(VEC.tensor_tensor(out=wKt[:], in0=kTM[:, cch, :, :],
                                   in1=wpr[:].unsqueeze(2).broadcast_to([128, H, DH]), op=ALU.mult))
            VEC.tensor_copy(Hb[:, cch, :, :].rearrange("p a b -> p (a b)"),
                            Hm[:, 0, :, :].rearrange("p a b -> p (a b)"))
            vinc(VEC.tensor_copy(Hbv[:, cch, :, :].rearrange("p a b -> p (a b)"),
                                 Hm[:, 1, :, :].rearrange("p a b -> p (a b)")))
            TEN.wait_ge(sV, nV[0])
            for h in HORD:
                pb = (h % 2) * 64
                TEN.matmul(pS[pb:pb + 64, h, 0:DH], wKt[:, h, :], kTM[:, cch, h, :],
                           start=True, stop=True, tile_position=(0, pb))
            for h in HORD:
                pb = (h % 2) * 64
                TEN.matmul(pAp[pb:pb + 64, h, :], wKt[:, h, :], vTM[:, cch, h, :],
                           start=True, stop=True, tile_position=(0, pb))
            tinc()
            VEC.wait_ge(sT, nT[0])
            for h in range(H):
                pb = (h % 2) * 64
                VEC.tensor_tensor(out=Hm[pb:pb + 64, 0, h // 2, :], in0=Hm[pb:pb + 64, 0, h // 2, :],
                                  in1=dece[pb:pb + 64, h:h + 1].broadcast_to([64, DH]), op=ALU.mult)
                VEC.tensor_tensor(out=Hm[pb:pb + 64, 1, h // 2, :], in0=Hm[pb:pb + 64, 1, h // 2, :],
                                  in1=dece[pb:pb + 64, h:h + 1].broadcast_to([64, DH]), op=ALU.mult)
            VEC.drain()
            for h in range(H):
                pb = (h % 2) * 64
                VEC.tensor_tensor(out=Hm[pb:pb + 64, 0, h // 2, :], in0=Hm[pb:pb + 64, 0, h // 2, :],
                                  in1=pS[pb:pb + 64, h, 0:DH], op=ALU.add)
                VEC.tensor_tensor(out=Hm[pb:pb + 64, 1, h // 2, :], in0=Hm[pb:pb + 64, 1, h // 2, :],
                                  in1=pAp[pb:pb + 64, h, :], op=ALU.add)
            vinc()
            TEN.wait_ge(sV, nV[0])
            for h in HORD:
                pb = (h % 2) * 64
                TEN.transpose(pPt[pb:pb + 64, cch % 2, h // 2, :], kTM[:, cch, h, :], ident[:],
                              tile_position=(0, pb))
            tinc()
            VEC.wait_ge(sT, nT[0])
            vinc(VEC.tensor_copy(kF[:, cch, :, :].rearrange("p a b -> p (a b)"),
                                 pPt[:, cch % 2, :, :].rearrange("p a b -> p (a b)")))
            TEN.wait_ge(sV, nV[0])
            for h in range(H):
                TEN.matmul(pS[:, h, :], onesb[:], grow[0:1, h * 128:(h + 1) * 128], start=True, stop=False)
                TEN.matmul(pS[:, h, :], onesb[:], grow[0:1, 2048 + h * 128:2048 + (h + 1) * 128], start=False, stop=True)
            tinc()
            VEC.wait_ge(sT, nT[0])
            for h in range(H):
                dst = (scr if h < 8 else scr2)[:, (h % 8) * 128:(h % 8 + 1) * 128]
                VEC.tensor_tensor(out=dst, in0=pS[:, h, :],
                                  in1=nGsh[:, cch, h:h + 1].broadcast_to([128, 128]), op=ALU.add)
                VEC.tensor_tensor(out=dst, in0=dst,
                                  in1=lnb_t[:, cch, h:h + 1].broadcast_to([128, 128]), op=ALU.min)
            vinc()
            SCA.wait_ge(sV, nV[0])
            for h in range(H):
                buf_ = (scr if h < 8 else scr2)[:, (h % 8) * 128:(h % 8 + 1) * 128]
                SCA.activation(out=buf_, in_=buf_, func=AF.Exp)
            scinc()
            VEC.wait_ge(sSc, nSc[0])
            for h in range(H):
                buf_ = (scr if h < 8 else scr2)[:, (h % 8) * 128:(h % 8 + 1) * 128]
                VEC.tensor_tensor(out=M_s[:, cch, HORD.index(h), :], in0=buf_, in1=trim[:], op=ALU.mult)
            vinc()

        barrier()
        VEC.sem_clear(cT1); VEC.sem_clear(cT2); VEC.sem_clear(cT3)
        VEC.sem_clear(cPt); VEC.sem_clear(cSh); VEC.sem_clear(cAp); VEC.sem_clear(cP)
        VEC.sem_inc(cP, 4)
        VEC.sem_inc(cPt, 2)
        barrier()

        if stage < 5:
            continue
        import contextlib as _ctx
        _unroll = int(os.environ.get("K_UNROLL", 0))
        if not _unroll:
            TEN.reg_mov(rT1, 1); TEN.reg_mov(rT3, 3)
            TEN.reg_mov(rT0, 0); TEN.reg_mov(rTs, 1)
        with (_ctx.nullcontext(None) if _unroll else nc.tensor.Fori(0, cg_iters)) as it:
            for _it2 in (range(cg_iters) if _unroll else [it]):
              for cch in range(NTB):
                if _unroll:
                    step = _it2 * 4 + cch
                    w_p1, w_p3, w_p0 = step + 1, step + 3, step
                    w_sh = lambda g_, s_=step: s_ * 4 + g_ + 1
                else:
                    w_p1, w_p3, w_p0 = rT1, rT3, rT0
                    w_sh = lambda g_: rTs
                TEN.wait_ge(cP, w_p1)
                TEN.wait_ge(cPt, w_p1)
                for h in HORD:
                    pb = (h % 2) * 64
                    TEN.transpose(pPt[pb:pb + 64, cch % 2, h // 2, :], P_s[:, cch, h, :],
                                  ident[:], tile_position=(0, pb))
                _last["T"].then_inc(cT1, 1)
                if cgl >= 3:
                    TEN.wait_ge(cPt, w_p3)
                    if cgl >= 6:
                        TEN.wait_ge(cAp, w_p0)
                    for i, h in enumerate(HORD):
                        pb = (h % 2) * 64
                        TEN.matmul(pS[:, i, :], kF[pb:pb + 64, cch, h // 2, :],
                                   Ptf[pb:pb + 64, cch % 2, h // 2, :],
                                   start=True, stop=True, tile_position=(pb, 0))
                        if i % 4 == 3:
                            _last["T"].then_inc(cT2, 1)
                if cgl >= 5:
                    for i, h in enumerate(HORD):
                        if i % 4 == 0:
                            TEN.wait_ge(cSh, w_sh(i // 4))
                            if not _unroll:
                                TEN.reg_add(rTs, rTs, 1)
                        TEN.matmul(pAp[:, h, :], Sh_s[:, cch % 2, i, :], kTM[:, cch, h, :],
                                   start=True, stop=True)
                    for i, h in enumerate(HORD):
                        pb = (h % 2) * 64
                        TEN.matmul(pS[:, i, 0:DH], Ptf[pb:pb + 64, cch % 2, h // 2, :],
                                   Hb[pb:pb + 64, cch, h // 2, :],
                                   start=True, stop=True, tile_position=(pb, 0))
                    _last["T"].then_inc(cT3, 1)
                if not _unroll:
                    TEN.reg_add(rT1, rT1, 1); TEN.reg_add(rT3, rT3, 1)
                    TEN.reg_add(rT0, rT0, 1)

        if not _unroll:
            SCA.reg_mov(rS1, 1)
        with (_ctx.nullcontext(None) if _unroll else nc.scalar.Fori(0, cg_iters)) as it:
            for _it2 in (range(cg_iters) if _unroll else [it]):
              for cch in range(NTB):
                s_p1 = (_it2 * 4 + cch + 1) if _unroll else rS1
                SCA.wait_ge(cT1, s_p1)
                SCA.activation(out=Ptf[:, cch % 2, :, :].rearrange("p a b -> p (a b)"),
                               in_=pPt[:, cch % 2, :, :].rearrange("p a b -> p (a b)"),
                               func=AF.Copy)
                _last["S"].then_inc(cPt, 1)
                if not _unroll:
                    SCA.reg_add(rS1, rS1, 1)

        if not _unroll:
            VEC.reg_mov(rV1, 1); VEC.reg_mov(rVs, 1)
        with (_ctx.nullcontext(None) if _unroll else nc.vector.Fori(0, cg_iters)) as it:
            for _it2 in (range(cg_iters) if _unroll else [it]):
              for cch in range(NTB):
                if _unroll:
                    step = _it2 * 4 + cch
                    v_p1 = step + 1
                    v_sh = lambda g_, s_=step: s_ * 4 + g_ + 1
                else:
                    v_p1 = rV1
                    v_sh = lambda g_: rVs

                if cgl < 4:
                    if not _unroll:
                        VEC.reg_add(rV1, rV1, 1)
                    VEC.memset(junkb[0:1, 0:1], 0.0).then_inc(cP, 1)
                    continue
                for g in range(4):
                    VEC.wait_ge(cT2, v_sh(g))
                    if not _unroll:
                        VEC.reg_add(rVs, rVs, 1)
                    VEC.tensor_tensor(out=Sh_s[:, cch % 2, 4 * g:4 * g + 4, :],
                                      in0=pS[:, 4 * g:4 * g + 4, :],
                                      in1=M_s[:, cch, 4 * g:4 * g + 4, :], op=ALU.mult)
                    _last["V"].then_inc(cSh, 1)
                if cgl < 6:
                    if not _unroll:
                        VEC.reg_add(rV1, rV1, 1)
                    VEC.memset(junkb[0:1, 0:1], 0.0).then_inc(cP, 1)
                    continue
                VEC.wait_ge(cT3, v_p1)
                if not _unroll:
                    VEC.reg_add(rV1, rV1, 1)
                VEC.tensor_tensor(out=TlP[:].rearrange("p h j -> p (h j)"),
                                  in0=P_s[:, cch, :, :].rearrange("p h j -> p (h j)"),
                                  in1=lbc[:], op=ALU.mult)
                VEC.tensor_tensor(out=Ap_s[:, 0, :, :].rearrange("p h j -> p (h j)"),
                                  in0=pAp_f[:], in1=TlP[:].rearrange("p h j -> p (h j)"), op=ALU.add)
                VEC.tensor_tensor(out=scr2[:, 0:D].rearrange("p (h j) -> p h j", h=H),
                                  in0=pS[:, :, 0:DH],
                                  in1=dec_sh[:, cch, :].unsqueeze(2).broadcast_to([128, H, DH]),
                                  op=ALU.mult)
                Ap_eo = Ap_s[:, 0, :, :].rearrange("p (a b) j -> p a b j", b=2)
                VEC.tensor_tensor(out=Ap_eo[:, :, 0, :], in0=Ap_eo[:, :, 0, :],
                                  in1=scr2[:, 0:512].rearrange("p (a j) -> p a j", a=8), op=ALU.add)
                VEC.tensor_tensor(out=Ap_eo[:, :, 1, :], in0=Ap_eo[:, :, 1, :],
                                  in1=scr2[:, 512:1024].rearrange("p (a j) -> p a j", a=8), op=ALU.add)
                _last["V"].then_inc(cAp, 1)
                if cgl < 7:
                    VEC.memset(junkb[0:1, 0:1], 0.0).then_inc(cP, 1)
                    continue
                VEC.tensor_tensor(out=scr2b[:].rearrange("p (h j) -> p h j", h=H),
                                  in0=Ap_s[:, 0, :, :], in1=P_s[:, cch, :, :], op=ALU.mult)
                VEC.tensor_reduce(out=pap_t[:, cch, :].unsqueeze(2),
                                  in_=scr2b[:].rearrange("p (h j) -> p h j", h=H),
                                  axis=mybir.AxisListType.X, op=ALU.add)
                spacer()
                VEC.tensor_scalar(out=t16a[:], in0=pap_t[:, cch, :], scalar1=1e-12, scalar2=None, op0=ALU.add)
                VEC.tensor_scalar(out=t16c[:], in0=rsA[:, cch, :], scalar1=1e-12, scalar2=None, op0=ALU.add)
                spacer()
                VEC.reciprocal(out=t16b[:], in_=t16a[:])
                VEC.reciprocal(out=t16d[:], in_=t16c[:])
                spacer()
                VEC.tensor_tensor(out=alp_t[:], in0=rsA[:, cch, :], in1=t16b[:], op=ALU.mult)
                spacer()
                VEC.tensor_tensor(out=scr[:, 0:D].rearrange("p (h j) -> p h j", h=H), in0=P_s[:, cch, :, :],
                                  in1=alp_t[:].unsqueeze(2).broadcast_to([128, H, DH]), op=ALU.mult)
                VEC.tensor_tensor(out=X_s[:, cch, :, :], in0=X_s[:, cch, :, :],
                                  in1=scr[:, 0:D].rearrange("p (h j) -> p h j", h=H), op=ALU.add)
                VEC.tensor_tensor(out=Ap_s[:, 0, :, :], in0=Ap_s[:, 0, :, :],
                                  in1=alp_t[:].unsqueeze(2).broadcast_to([128, H, DH]), op=ALU.mult)
                VEC.tensor_tensor(out=R_s[:, cch, :, :], in0=R_s[:, cch, :, :],
                                  in1=Ap_s[:, 0, :, :], op=ALU.subtract)
                VEC.tensor_tensor(out=scr2b[:].rearrange("p (h j) -> p h j", h=H),
                                  in0=R_s[:, cch, :, :], in1=R_s[:, cch, :, :], op=ALU.mult)
                VEC.tensor_reduce(out=rsB[:, cch, :].unsqueeze(2),
                                  in_=scr2b[:].rearrange("p (h j) -> p h j", h=H),
                                  axis=mybir.AxisListType.X, op=ALU.add)
                spacer()
                VEC.tensor_tensor(out=bet_t[:], in0=rsB[:, cch, :], in1=t16d[:], op=ALU.mult)
                VEC.tensor_copy(rsA[:, cch, :], rsB[:, cch, :])
                spacer()
                VEC.tensor_tensor(out=scr[:, 0:D].rearrange("p (h j) -> p h j", h=H), in0=P_s[:, cch, :, :],
                                  in1=bet_t[:].unsqueeze(2).broadcast_to([128, H, DH]), op=ALU.mult)
                VEC.tensor_tensor(out=P_s[:, cch, :, :], in0=scr[:, 0:D].rearrange("p (h j) -> p h j", h=H),
                                  in1=R_s[:, cch, :, :], op=ALU.add)
                _last["V"].then_inc(cP, 1)

        barrier()

        if stage < 6:
            continue
        for cch in range(NTB):
            VEC.tensor_copy(P_s[:, cch, :, :].rearrange("p h j -> p (h j)"),
                            X_s[:, cch, :, :].rearrange("p h j -> p (h j)"))
            vinc()
            TEN.wait_ge(sV, nV[0])
            for h in HORD:
                pb = (h % 2) * 64
                TEN.transpose(pPt[pb:pb + 64, cch % 2, h // 2, :], P_s[:, cch, h, :],
                              ident[:], tile_position=(0, pb))
            tinc()
            VEC.wait_ge(sT, nT[0])
            vinc(VEC.tensor_copy(Ptf[:, cch % 2, :, :].rearrange("p a b -> p (a b)"),
                                 pPt[:, cch % 2, :, :].rearrange("p a b -> p (a b)")))
            TEN.wait_ge(sV, nV[0])
            for i, h in enumerate(HORD):
                pb = (h % 2) * 64
                TEN.matmul(pS[:, i, :], kF[pb:pb + 64, cch, h // 2, :],
                           Ptf[pb:pb + 64, cch % 2, h // 2, :],
                           start=True, stop=True, tile_position=(pb, 0))
            tinc()
            VEC.wait_ge(sT, nT[0])
            for g in range(4):
                VEC.tensor_tensor(out=Sh_s[:, cch % 2, 4 * g:4 * g + 4, :],
                                  in0=pS[:, 4 * g:4 * g + 4, :],
                                  in1=M_s[:, cch, 4 * g:4 * g + 4, :], op=ALU.mult)
            vinc()
            TEN.wait_ge(sV, nV[0])
            for i, h in enumerate(HORD):
                TEN.matmul(pAp[:, h, :], Sh_s[:, cch % 2, i, :], vTM[:, cch, h, :],
                           start=True, stop=True)
            for i, h in enumerate(HORD):
                pb = (h % 2) * 64
                TEN.matmul(pS[:, i, 0:DH], Ptf[pb:pb + 64, cch % 2, h // 2, :],
                           Hbv[pb:pb + 64, cch, h // 2, :],
                           start=True, stop=True, tile_position=(pb, 0))
            tinc()
            VEC.wait_ge(sT, nT[0])
            VEC.tensor_copy(Ap_s[:, 0, :, :].rearrange("p h j -> p (h j)"), pAp_f[:])
            for i, h in enumerate(HORD):
                VEC.scalar_tensor_tensor(out=Ap_s[:, 0, h, :], in0=pS[:, i, 0:DH],
                                         scalar=dec_s[:, cch, h:h + 1],
                                         in1=Ap_s[:, 0, h, :], op0=ALU.mult, op1=ALU.add)
            VEC.tensor_tensor(out=Ap_s[:, 0, :, :].rearrange("p h j -> p (h j)"),
                              in0=Ap_s[:, 0, :, :].rearrange("p h j -> p (h j)"),
                              in1=gate_s[:, cch, :, :].rearrange("p h j -> p (h j)"), op=ALU.mult)
            VEC.drain()
            VEC.tensor_tensor(out=scr2[:, 0:D].rearrange("p (h j) -> p h j", h=H),
                              in0=Ap_s[:, 0, :, :], in1=Ap_s[:, 0, :, :], op=ALU.mult)
            VEC.drain()
            VEC.tensor_reduce(out=qss[:].unsqueeze(2),
                              in_=scr2[:, 0:D].rearrange("p (h j) -> p h j", h=H),
                              axis=mybir.AxisListType.X, op=ALU.add)
            VEC.drain()
            VEC.tensor_scalar(out=qss[:], in0=qss[:], scalar1=1.0 / DH, scalar2=1e-5, op0=ALU.mult, op1=ALU.add)
            vinc()
            SCA.wait_ge(sV, nV[0])
            scinc(SCA.activation(out=t16a[:], in_=qss[:], func=AF.Sqrt))
            VEC.wait_ge(sSc, nSc[0])
            VEC.reciprocal(out=t16b[:], in_=t16a[:])
            VEC.drain()
            vinc(VEC.tensor_tensor(out=xnT[:, cch, :].rearrange("p (h j) -> p h j", h=H),
                                   in0=Ap_s[:, 0, :, :],
                                   in1=t16b[:].unsqueeze(2).broadcast_to([128, H, DH]), op=ALU.mult))

        if stage < 7:
            continue
        ogF = u8k[:].rearrange("p (db t) -> p db t", db=8)
        nT_base = nT[0]; nV_base = nV[0]
        for tb in range(NTB):
            for db in range(8):
                idx = tb * 8 + db
                TEN.wait_ge(sV, nV_base if idx < 2 else nV_base + idx - 1)
                tinc(TEN.transpose(pPt[:, idx % 2, 0, :], xnT[:, tb, db * 128:(db + 1) * 128],
                                   ident[:]))
        for tb in range(NTB):
            for db in range(8):
                idx = tb * 8 + db
                VEC.wait_ge(sT, nT_base + idx + 1)
                vinc(VEC.tensor_copy(ogF[:, db, tb * 128:(tb + 1) * 128], pPt[:, idx % 2, 0, :]))
        s_t = X_s[:].rearrange("p c h j -> p c (h j)")
        for ch in range(2):
            for tb in range(NTB):
                bank = pS[:, tb * 4:(tb + 1) * 4, :].rearrange("p a t -> p (a t)")
                TEN.wait_ge(sV, nV[0])
                for kh in range(2):
                    hk = 6 + kh
                    wait_half(hk)
                    for d4 in range(4):
                        db = kh * 4 + d4
                        TEN.matmul(bank[:, 0:512], ogF[:, db, tb * 128:(tb + 1) * 128],
                                   wr[:, half_slots(hk)[d4], ch * 512:(ch + 1) * 512],
                                   start=(db == 0), stop=(db == 7))
                tinc()
                VEC.wait_ge(sT, nT[0])
                vinc(VEC.tensor_copy(s_t[:, tb, ch * 512:(ch + 1) * 512], bank[:, 0:512]))
        tw_done()

        if stage < 8:
            continue
        acc = R_s[:].rearrange("p c h j -> p c (h j)")
        for e in range(8):
            for ch in range(2):
                for tb in range(NTB):
                    TEN.wait_ge(sV, nV[0])
                    for kh in range(2):
                        hk = 8 + 2 * e + kh
                        wait_half(hk)
                        for d4 in range(4):
                            db = kh * 4 + d4
                            TEN.matmul(pS_f[:, ch * 512:(ch + 1) * 512],
                                       xnF[:, db, tb * 128:(tb + 1) * 128],
                                       wr[:, half_slots(hk)[d4], ch * 512:(ch + 1) * 512],
                                       start=(db == 0), stop=(db == 7))
                    tinc()
                    VEC.wait_ge(sT, nT[0])
                    if e == 0:
                        vinc(VEC.tensor_scalar(out=acc[:, tb, ch * 512:(ch + 1) * 512],
                                               in0=pS_f[:, ch * 512:(ch + 1) * 512],
                                               scalar1=wfin[:, tb, e:e + 1], scalar2=None, op0=ALU.mult))
                    else:
                        vinc(VEC.scalar_tensor_tensor(out=acc[:, tb, ch * 512:(ch + 1) * 512],
                                                      in0=pS_f[:, ch * 512:(ch + 1) * 512],
                                                      scalar=wfin[:, tb, e:e + 1],
                                                      in1=acc[:, tb, ch * 512:(ch + 1) * 512],
                                                      op0=ALU.mult, op1=ALU.add))
            tw_done()

        for tb in range(NTB):
            VEC.drain()
            VEC.tensor_copy(wf8[:], wfin[:, tb, :])
            vinc()
            TEN.wait_ge(sV, nV[0])
            tinc(TEN.transpose(pPt[0:8, 0, 0, :], wf8[:], ident[:]))
            VEC.wait_ge(sT, nT[0])
            vinc(VEC.tensor_copy(wfT[:], pPt[0:8, 0, 0, :]))
            TEN.wait_ge(sV, nV[0])
            for ch in range(2):
                TEN.matmul(pAp_f[:, ch * 512:(ch + 1) * 512], wfT[:], eb_s[:, ch * 512:(ch + 1) * 512],
                           start=True, stop=True)
            tinc()
            VEC.wait_ge(sT, nT[0])
            VEC.tensor_tensor(out=scr[:, 0:D], in0=acc[:, tb, :], in1=pAp_f[:], op=ALU.add)
            vinc()
            SYN.wait_ge(sV, nV[0])
            dmas(x_s[:], x_in[t0 + tb * 128: t0 + (tb + 1) * 128, :])
            SCA.wait_ge(sV, nV[0])
            scinc(SCA.activation(out=scr2[:, 0:D], in_=scr[:, 0:D], func=AF.Silu))
            VEC.wait_ge(sSc, nSc[0])
            VEC.tensor_tensor(out=scr[:, 0:D], in0=s_t[:, tb, :], in1=scr2[:, 0:D], op=ALU.mult)
            VEC.wait_ge(dmaS, 16 * nS[0])
            vinc(VEC.tensor_tensor(out=scr2[:, 0:D], in0=x_s[:], in1=scr[:, 0:D], op=ALU.add))
            SYN.wait_ge(sV, nV[0])
            dmas(out_t[t0 + tb * 128: t0 + (tb + 1) * 128, :], scr2[:, 0:D])

        barrier()

    dump = os.environ.get("K_DUMP", "")
    if dump:
        src_map = dict(M_s=M_s, P_s=P_s, kTM=kTM, vTM=vTM, R_s=R_s, X_s=X_s,
                       xnT=xnT, gate_s=gate_s, Hm=Hm, dec_s=dec_s, Gc=Gc,
                       lbc=lbc, Hb=Hb, Hbv=Hbv, Sh_s=Sh_s, Ap_s=Ap_s, rsA=rsA, Ptf=Ptf, scr=scr, scr2=scr2, qss=qss, t1F=t1F,
                       u8k=u8k, beta_t=beta_t, wfin=wfin, xnF=xnF)
        t_ = src_map[dump]
        ap_ = t_[:]
        if ap_.ndim > 2:
            ap_ = ap_.rearrange("p " + " ".join(chr(97 + i) for i in range(ap_.ndim - 1))
                                + " -> p (" + " ".join(chr(97 + i) for i in range(ap_.ndim - 1)) + ")")
        n_ = min(ap_.shape[1], 8192)
        npart = ap_.shape[0]
        for off in range(0, n_, 1024):
            w_ = min(1024, n_ - off)
            VEC.tensor_copy(scr[0:npart, 0:w_], ap_[:, off:off + w_])
            vinc()
            SYN.wait_ge(sV, nV[0])
            dmas(dbg_t[0:npart, off:off + w_], scr[0:npart, 0:w_])
            SYN.wait_ge(dmaS, 16 * nS[0])
            VEC.wait_ge(dmaS, 16 * nS[0])
    SYN.wait_ge(dmaS, 16 * nS[0])
    barrier()
    es.close()
    return nc


_CONSTS = None


def _consts():
    global _CONSTS
    if _CONSTS is None:
        ident = np.eye(128, dtype=np.float32)
        triu = (np.arange(128)[:, None] <= np.arange(128)[None, :]).astype(np.float32)
        _CONSTS = dict(cid16=ident.astype(BF), cidf=ident, ctriu=triu,
                       ctrim=triu.astype(BF))
    return _CONSTS


def make_inputs(x, inp):
    shared = dict(
        wq=np.ascontiguousarray(inp["L0_attn_Wq"], np.float32),
        wk=np.ascontiguousarray(inp["L0_attn_Wk"], np.float32),
        wv=np.ascontiguousarray(inp["L0_attn_Wv"], np.float32),
        wf=np.ascontiguousarray(inp["L0_attn_Wf"], np.float32),
        wb=np.ascontiguousarray(inp["L0_attn_Wbet"], np.float32),
        wo=np.ascontiguousarray(inp["L0_attn_Wo"], np.float32),
        delta=np.ascontiguousarray(inp["L0_attn_delta"], np.float32).reshape(1, H),
        lambp=np.ascontiguousarray(inp["L0_attn_lamb_params"], np.float32).reshape(1, D),
        nw=np.ascontiguousarray(inp["L0_attn_norm_weight"], np.float32).reshape(1, D),
        g1=np.ascontiguousarray(inp["L0_attn_gate_w1"], np.float32),
        g2=np.ascontiguousarray(inp["L0_attn_gate_w2"], np.float32),
        gw=np.ascontiguousarray(inp["L0_ffn_gate_w"], np.float32),
        gb=np.ascontiguousarray(inp["L0_ffn_gate_b"], np.float32).reshape(1, 8),
        we=np.ascontiguousarray(inp["L0_ffn_expert_w"], np.float32),
        eb=np.ascontiguousarray(inp["L0_ffn_expert_b"], np.float32),
        **_consts(),
    )
    return [dict(x=np.ascontiguousarray(x[i]), **shared) for i in range(x.shape[0])]


def kernel(L0_attn_x, **kw):
    x = np.ascontiguousarray(np.asarray(L0_attn_x, dtype=np.float32))
    inp = dict(kw); inp["L0_attn_x"] = x
    b = x.shape[0]
    nc = build()
    in_maps = make_inputs(x, inp)
    res = run_bass_kernel_spmd(nc, in_maps, core_ids=list(range(b)))
    return np.stack([np.asarray(r["out"]) for r in res.results], axis=0)


if __name__ == "__main__":
    nc = build(512, 4)
    print("built ok")


# version 10 (final)
# speedup vs baseline: 1.0104x; 1.0104x over previous
import contextlib
import numpy as np
import ml_dtypes

import concourse.bass as bass
import concourse.mybir as mybir
from concourse.bass_utils import run_bass_kernel_spmd

F32 = mybir.dt.float32
BF16 = mybir.dt.bfloat16
AF = mybir.ActivationFunctionType
ALU = mybir.AluOpType
BF = ml_dtypes.bfloat16

N = 8192
D = 1024
H = 16
DH = 64
CG_ITERS = 30
SEG = 512
NTB = SEG // 128


def build(n_tokens=N, cg_iters=CG_ITERS, stage=99):
    import os
    stage = int(os.environ.get("K_STAGE", stage))
    p0 = int(os.environ.get("K_P0", 4))
    cgl = int(os.environ.get("K_CG", 7))
    nseg = n_tokens // SEG
    nc = bass.Bass("TRN2", target_bir_lowering=False, detect_race_conditions=False)

    x_in = nc.dram_tensor("x", [n_tokens, D], F32, kind="ExternalInput").ap()
    wq = nc.dram_tensor("wq", [D, D], F32, kind="ExternalInput").ap()
    wk = nc.dram_tensor("wk", [D, D], F32, kind="ExternalInput").ap()
    wv = nc.dram_tensor("wv", [D, D], F32, kind="ExternalInput").ap()
    wf = nc.dram_tensor("wf", [D, H], F32, kind="ExternalInput").ap()
    wb = nc.dram_tensor("wb", [D, H], F32, kind="ExternalInput").ap()
    wo = nc.dram_tensor("wo", [D, D], F32, kind="ExternalInput").ap()
    delta = nc.dram_tensor("delta", [1, H], F32, kind="ExternalInput").ap()
    lambp = nc.dram_tensor("lambp", [1, D], F32, kind="ExternalInput").ap()
    nw_in = nc.dram_tensor("nw", [1, D], F32, kind="ExternalInput").ap()
    g1 = nc.dram_tensor("g1", [D, DH], F32, kind="ExternalInput").ap()
    g2 = nc.dram_tensor("g2", [DH, D], F32, kind="ExternalInput").ap()
    gw = nc.dram_tensor("gw", [D, 8], F32, kind="ExternalInput").ap()
    gb = nc.dram_tensor("gb", [1, 8], F32, kind="ExternalInput").ap()
    we = nc.dram_tensor("we", [8, D, D], F32, kind="ExternalInput").ap()
    eb = nc.dram_tensor("eb", [8, D], F32, kind="ExternalInput").ap()
    cid16 = nc.dram_tensor("cid16", [128, 128], BF16, kind="ExternalInput").ap()
    cidf = nc.dram_tensor("cidf", [128, 128], F32, kind="ExternalInput").ap()
    ctriu = nc.dram_tensor("ctriu", [128, 128], F32, kind="ExternalInput").ap()
    ctrim = nc.dram_tensor("ctrim", [128, 128], BF16, kind="ExternalInput").ap()

    out_t = nc.dram_tensor("out", [n_tokens, D], F32, kind="ExternalOutput").ap()
    dbg_t = (nc.dram_tensor("dbg", [128, 8192], F32, kind="ExternalOutput").ap()
             if os.environ.get("K_DUMP") else None)

    wq_b = nc.dram_tensor("wq_b", [D, D], BF16, kind="Internal").ap()
    wk_b = nc.dram_tensor("wk_b", [D, D], BF16, kind="Internal").ap()
    wv_b = nc.dram_tensor("wv_b", [D, D], BF16, kind="Internal").ap()
    wo_b = nc.dram_tensor("wo_b", [D, D], BF16, kind="Internal").ap()
    we_b = nc.dram_tensor("we_b", [8, D, D], BF16, kind="Internal").ap()
    lam_d = nc.dram_tensor("lam_d", [1, D], F32, kind="Internal").ap()

    es = contextlib.ExitStack()
    sb = lambda n_, s, d: es.enter_context(nc.sbuf_tensor(n_, s, d))
    psumt = lambda n_, s, d=F32: es.enter_context(nc.psum_tensor(n_, s, d))
    sem = lambda n_: es.enter_context(nc.semaphore(n_))

    x_s = sb("x_s", [128, D], F32)
    xnF = sb("xnF", [128, 8, SEG], BF16)
    xnT = sb("xnT", [128, NTB, D], BF16)
    R_s = sb("R_s", [128, NTB, H, DH], F32)
    X_s = sb("X_s", [128, NTB, H, DH], F32)
    P_s = sb("P_s", [128, NTB, H, DH], BF16)
    Ptf = sb("Ptf", [128, 2, 8, 128], BF16)
    kTM = sb("kTM", [128, NTB, H, DH], BF16)
    vTM = sb("vTM", [128, NTB, H, DH], BF16)
    u8k = sb("u8k", [128, 4096], BF16)
    gate_s = sb("gate_s", [128, NTB, H, DH], BF16)
    M_s = sb("M_s", [128, NTB, H, 128], BF16)
    Sh_s = sb("Sh_s", [128, 2, H, 128], BF16)
    Ap_s = sb("Ap_s", [128, 1, H, DH], F32)
    TlP = sb("TlP", [128, H, DH], BF16)
    lbc = sb("lbc", [128, D], BF16)
    Hm = sb("Hm", [128, 2, 8, DH], F32)
    Hb = sb("Hb", [128, NTB, 8, DH], BF16)
    Hbv = sb("Hbv", [128, NTB, 8, DH], BF16)
    wKt = sb("wKt", [128, H, DH], BF16)
    wr = sb("wr", [128, 12, D], BF16)
    g1_s = sb("g1_s", [128, 8, DH], BF16)
    g2_s = sb("g2_s", [64, D], BF16)
    eb_s = sb("eb_s", [8, D], BF16)
    fbw_s = sb("fbw_s", [128, 8, 40], BF16)
    ident = sb("ident", [128, 128], BF16)
    identf = sb("identf", [128, 128], F32)
    triu = sb("triu", [128, 128], F32)
    trim = sb("trim", [128, 128], BF16)
    onesc = sb("onesc", [1, 128], F32)
    nw_s = sb("nw_s", [128, 8], F32)
    delta_s = sb("delta_s", [128, H], F32)
    gbb = sb("gbb", [128, 8], F32)
    lrow = sb("lrow", [1, D], F32)
    lrow2 = sb("lrow2", [1, D], F32)
    grow = sb("grow", [1, 4096], BF16)
    fln = sb("fln", [128, NTB, H], F32)
    Gc = sb("Gc", [128, NTB, H], F32)
    nGsh = sb("nGsh", [128, NTB, H], F32)
    lnb_t = sb("lnb_t", [128, NTB, H], F32)
    beta_t = sb("beta_t", [128, NTB, H], F32)
    dec_s = sb("dec_s", [128, NTB, H], F32)
    dece = sb("dece", [128, H], F32)
    wpr = sb("wpr", [128, H], F32)
    GFs = sb("GFs", [32, 128], BF16)
    fhl = sb("fhl", [128, NTB, 32], BF16)
    ghl = sb("ghl", [128, 32], BF16)
    onesall = sb("onesall", [128, 128], BF16)
    onesb = sb("onesb", [1, 128], BF16)
    rsA = sb("rsA", [128, NTB, H], F32)
    rsB = sb("rsB", [128, NTB, H], F32)
    pap_t = sb("pap_t", [128, NTB, H], F32)
    alp_t = sb("alp_t", [128, H], F32)
    nalp_t = sb("nalp_t", [128, H], F32)
    bet_t = sb("bet_t", [128, H], F32)
    t16a = sb("t16a", [128, H], F32)
    t16b = sb("t16b", [128, H], F32)
    t16c = sb("t16c", [128, H], F32)
    t16d = sb("t16d", [128, H], F32)
    junkb = sb("junkb", [128, DH], BF16)
    junkc = sb("junkc", [128, 256], F32)
    dec_sh = sb("dec_sh", [128, NTB, H], F32)
    dech = sb("dech", [128, 8], F32)
    probs = sb("probs", [128, NTB, 8], F32)
    wfin = sb("wfin", [128, NTB, 8], F32)
    wf8 = sb("wf8", [128, 8], BF16)
    wfT = sb("wfT", [8, 128], BF16)
    t1F = sb("t1F", [64, SEG], BF16)
    scr = sb("scr", [128, 1024], F32)
    scr2 = sb("scr2", [128, 1024], F32)
    qss = sb("qss", [128, H], F32)

    pS = psumt("pS", [128, H, 128], F32)
    pPt = psumt("pPt", [128, 2, 8, 128], BF16)
    pAp = psumt("pAp", [128, H, DH], F32)
    pS_f = pS[:].rearrange("p h t -> p (h t)")
    pAp_f = pAp[:].rearrange("p h j -> p (h j)")

    dmaS = sem("dmaS"); dmaW = sem("dmaW")
    sT = sem("sT"); sV = sem("sV"); sSc = sem("sSc"); sTW = sem("sTW")
    cT1 = sem("cT1"); cT2 = sem("cT2"); cT3 = sem("cT3")
    cPt = sem("cPt"); cSh = sem("cSh"); cAp = sem("cAp"); cP = sem("cP")

    _TRACKED = {"matmul", "transpose", "tensor_copy", "tensor_tensor",
                "tensor_scalar", "scalar_tensor_tensor", "tensor_single_scalar",
                "activation", "memset", "memzero", "reciprocal", "reduce_max",
                "reduce_sum", "tensor_reduce", "copy", "add", "mul", "nop"}
    _last = {}

    class _EP:
        def __init__(self, eng, key):
            self._e = eng; self._k = key
        def __getattr__(self, n):
            a = getattr(self._e, n)
            if n in _TRACKED:
                def f(*args, **kw):
                    r = a(*args, **kw)
                    _last[self._k] = r
                    return r
                return f
            return a

    HORD = list(range(0, H, 2)) + list(range(1, H, 2))
    TEN, VEC, SCA = _EP(nc.tensor, "T"), _EP(nc.vector, "V"), _EP(nc.scalar, "S")
    SYN, GPS = nc.sync, nc.gpsimd
    rT1 = TEN.alloc_register(); rT3 = TEN.alloc_register()
    rT0 = TEN.alloc_register(); rTs = TEN.alloc_register()
    rV1 = VEC.alloc_register(); rVs = VEC.alloc_register()
    rS1 = SCA.alloc_register()
    _BARR = [mybir.EngineType.PE, mybir.EngineType.DVE,
             mybir.EngineType.Activation, mybir.EngineType.SP]

    def barrier():
        nc.multi_engine_barrier(_BARR)
    nS = [0]; nW = [0]; nT = [0]; nV = [0]; nSc = [0]; nTW = [0]

    def dmas(out, in_, slow=False):
        SYN.dma_start(out=out, in_=in_, allow_slow_non_contiguous=slow).then_inc(dmaS, 16)
        nS[0] += 1

    def dmaw(out, in_):
        GPS.dma_start(out=out, in_=in_).then_inc(dmaW, 16)
        nW[0] += 1

    def tinc(instr=None):
        (_last["T"] if instr is None else instr).then_inc(sT, 1); nT[0] += 1

    def vinc(instr=None):
        (_last["V"] if instr is None else instr).then_inc(sV, 1); nV[0] += 1

    def scinc(instr=None):
        (_last["S"] if instr is None else instr).then_inc(sSc, 1); nSc[0] += 1

    def spacer():
        VEC.tensor_scalar(out=junkc[:], in0=junkc[:], scalar1=1.0, scalar2=None, op0=ALU.mult)

    def tw_done():
        VEC.memset(junkb[0:1, 0:1], 0.0).then_inc(sTW, 1); nTW[0] += 1

    dmas(identf[:], cidf)
    dmas(triu[:], ctriu)
    dmas(ident[:], cid16)
    dmas(trim[:], ctrim)
    dmas(nw_s[:].rearrange("p b -> p b"), nw_in.rearrange("one (b p) -> p one b", p=128).rearrange("p one b -> p (one b)"), slow=True)
    dmas(lrow[:], lambp)
    dmas(lrow2[0:1, 0:H], delta)
    dmas(lrow2[0:1, H:H + 8], gb)
    VEC.wait_ge(dmaS, 16 * nS[0])
    VEC.memset(onesc[:], 1.0)
    VEC.memset(onesb[:], 1.0)
    VEC.memset(onesall[:], 1.0)
    VEC.memset(Hm[:].rearrange("p a b j -> p (a b j)"), 0.0)
    vinc(VEC.memset(junkb[:], 0.0))
    dmas(delta_s[:].unsqueeze(1), delta.partition_broadcast(128), slow=True)
    dmas(gbb[:].unsqueeze(1), gb.partition_broadcast(128), slow=True)

    if p0 < 1:
        barrier(); SYN.wait_ge(dmaS, 16 * nS[0]); barrier(); es.close(); return nc
    SCA.wait_ge(sV, nV[0])
    SCA.activation(out=scr[0:1, 0:D], in_=lrow[:], func=AF.Abs)
    SCA.drain()
    scinc(SCA.activation(out=scr[0:1, 0:D], in_=scr[0:1, 0:D], func=AF.Exp, scale=-1.0))
    VEC.wait_ge(sSc, nSc[0])
    VEC.tensor_scalar(out=scr[0:1, 0:D], in0=scr[0:1, 0:D], scalar1=1.0, scalar2=None, op0=ALU.add)
    vinc()
    SCA.wait_ge(sV, nV[0])
    scinc(SCA.activation(out=scr[0:1, 0:D], in_=scr[0:1, 0:D], func=AF.Ln))
    VEC.wait_ge(sSc, nSc[0])
    VEC.tensor_single_scalar(out=scr2[0:1, 0:D], in_=lrow[:], scalar=0.0, op=ALU.max)
    VEC.drain()
    VEC.tensor_tensor(out=scr[0:1, 0:D], in0=scr[0:1, 0:D], in1=scr2[0:1, 0:D], op=ALU.add)
    VEC.drain()
    VEC.tensor_scalar(out=lrow[0:1, :], in0=scr[0:1, 0:D], scalar1=0.25, scalar2=None, op0=ALU.add)
    vinc()
    if p0 < 2:
        barrier(); SYN.wait_ge(dmaS, 16 * nS[0]); barrier(); es.close(); return nc
    SYN.wait_ge(sV, nV[0])
    dmas(lam_d, lrow[:])
    dmas(scr[:].unsqueeze(1), lam_d.partition_broadcast(128), slow=True)
    VEC.wait_ge(dmaS, 16 * nS[0])
    vinc(VEC.tensor_copy(lbc[:], scr[:, 0:D]))

    if p0 < 3:
        barrier(); SYN.wait_ge(dmaS, 16 * nS[0]); barrier(); es.close(); return nc
    cast_jobs = []
    for db in range(8):
        cast_jobs.append((wq[db * 128:(db + 1) * 128, :], wq_b[db * 128:(db + 1) * 128, :], None))
        cast_jobs.append((wk[db * 128:(db + 1) * 128, :], wk_b[db * 128:(db + 1) * 128, :], None))
        cast_jobs.append((wv[db * 128:(db + 1) * 128, :], wv_b[db * 128:(db + 1) * 128, :], None))
        cast_jobs.append((wo[db * 128:(db + 1) * 128, :], wo_b[db * 128:(db + 1) * 128, :], db))
        for e in range(8):
            cast_jobs.append((we[e, db * 128:(db + 1) * 128, :], we_b[e, db * 128:(db + 1) * 128, :], None))
    for j, (src, dst, nwdb) in enumerate(cast_jobs):
        buf = scr[:, 0:D] if j % 2 == 0 else scr2[:, 0:D]
        stg = wr[:, j % 2, :]
        SYN.wait_ge(sV, nV[0])
        dmas(buf, src)
        VEC.wait_ge(dmaS, 16 * nS[0])
        if nwdb is None:
            vinc(VEC.tensor_copy(stg, buf))
        else:
            vinc(VEC.tensor_scalar(out=stg, in0=buf, scalar1=nw_s[:, nwdb:nwdb + 1],
                                   scalar2=None, op0=ALU.mult))
        SYN.wait_ge(sV, nV[0])
        dmas(dst, stg)
    if p0 < 4:
        barrier(); SYN.wait_ge(dmaS, 16 * nS[0]); barrier(); es.close(); return nc
    SYN.wait_ge(sV, nV[0])
    dmas(scr[:, 0:512].rearrange("p (b j) -> p b j", b=8), g1.rearrange("(b p) j -> p b j", p=128), slow=True)
    VEC.wait_ge(dmaS, 16 * nS[0])
    vinc(VEC.tensor_copy(g1_s[:].rearrange("p b j -> p (b j)"), scr[:, 0:512]))
    SYN.wait_ge(sV, nV[0])
    dmas(scr[0:64, 0:D], g2)
    VEC.wait_ge(dmaS, 16 * nS[0])
    vinc(VEC.tensor_copy(g2_s[:], scr[0:64, 0:D]))
    SYN.wait_ge(sV, nV[0])
    dmas(scr[0:8, 0:D], eb)
    VEC.wait_ge(dmaS, 16 * nS[0])
    vinc(VEC.tensor_copy(eb_s[:], scr[0:8, 0:D]))
    for db in range(8):
        SYN.wait_ge(sV, nV[0])
        dmas(scr2[:, 0:16], wf[db * 128:(db + 1) * 128, :])
        dmas(scr2[:, 16:32], wb[db * 128:(db + 1) * 128, :])
        dmas(scr2[:, 32:40], gw[db * 128:(db + 1) * 128, :])
        VEC.wait_ge(dmaS, 16 * nS[0])
        vinc(VEC.tensor_copy(fbw_s[:, db, :], scr2[:, 0:40]))

    barrier()

    HALVES_PER_SEG = 24

    def half_slots(k):
        return [(k * 4 + d4) % 12 for d4 in range(4)]

    for seg in range(nseg):
        t0 = seg * SEG
        half_base = seg * HALVES_PER_SEG
        ring_base = nTW[0]

        def wait_half(hk):
            TEN.wait_ge(dmaW, 16 * 4 * (half_base + hk + 1))

        if seg == 0 and stage >= 2:
            GPS.wait_ge(dmaS, 16 * nS[0])
        ring_seq = [(wq_b, 0), (wq_b, 1), (wk_b, 0), (wk_b, 1), (wv_b, 0), (wv_b, 1),
                    (wo_b, 0), (wo_b, 1)]
        for e in range(8):
            ring_seq.append((we_b[e], 0)); ring_seq.append((we_b[e], 1))
        n_halves = 0 if stage < 2 else (6 if stage < 7 else (8 if stage < 8 else 24))
        for k, (src, hf) in enumerate(ring_seq[:n_halves]):
            if k >= 3:
                GPS.wait_ge(sTW, ring_base + (k - 3) // 2 + 1)
            elif seg > 0:
                GPS.wait_ge(sTW, ring_base - 12 + (21 + k) // 2 + 1)
            for d4 in range(4):
                db = hf * 4 + d4
                dmaw(wr[:, half_slots(k)[d4], :], src[db * 128:(db + 1) * 128, :])

        if stage < 1:
            continue
        for tb in range(NTB):
            SYN.wait_ge(sV, nV[0])
            dmas(x_s[:], x_in[t0 + tb * 128: t0 + (tb + 1) * 128, :])
            VEC.wait_ge(dmaS, 16 * nS[0])
            xv = x_s[:]
            VEC.tensor_reduce(out=qss[:, 0:1], in_=xv, axis=mybir.AxisListType.X, op=ALU.add)
            VEC.drain()
            VEC.tensor_scalar(out=qss[:, 1:2], in0=qss[:, 0:1], scalar1=1.0 / D, scalar2=None, op0=ALU.mult)
            VEC.drain()
            VEC.tensor_tensor(out=scr[:, 0:D], in0=xv,
                              in1=qss[:, 1:2].broadcast_to([128, D]), op=ALU.subtract)
            VEC.scalar_tensor_tensor(out=scr2[:, 0:D], in0=scr[:, 0:D], scalar=1.0,
                                     in1=scr[:, 0:D], op0=ALU.mult, op1=ALU.mult,
                                     accum_out=qss[:, 2:3])
            VEC.drain()
            VEC.tensor_scalar(out=qss[:, 3:4], in0=qss[:, 2:3], scalar1=1.0 / D, scalar2=1e-5, op0=ALU.mult, op1=ALU.add)
            vinc()
            SCA.wait_ge(sV, nV[0])
            scinc(SCA.activation(out=qss[:, 4:5], in_=qss[:, 3:4], func=AF.Sqrt))
            VEC.wait_ge(sSc, nSc[0])
            VEC.reciprocal(out=qss[:, 5:6], in_=qss[:, 4:5])
            VEC.drain()
            vinc(VEC.tensor_tensor(out=xnT[:, tb, :], in0=scr[:, 0:D],
                                   in1=qss[:, 5:6].broadcast_to([128, D]), op=ALU.mult))

        nT_base = nT[0]; nV_base = nV[0]
        for tb in range(NTB):
            for db in range(8):
                idx = tb * 8 + db
                TEN.wait_ge(sV, nV_base if idx < 2 else nV_base + idx - 1)
                tinc(TEN.transpose(pPt[:, idx % 2, 0, :], xnT[:, tb, db * 128:(db + 1) * 128],
                                   ident[:]))
        for tb in range(NTB):
            for db in range(8):
                idx = tb * 8 + db
                VEC.wait_ge(sT, nT_base + idx + 1)
                vinc(VEC.tensor_copy(xnF[:, db, tb * 128:(tb + 1) * 128], pPt[:, idx % 2, 0, :]))

        if stage < 2:
            continue
        def proj_group(gidx, evac):
            for ch in range(2):
                for tb in range(NTB):
                    bank = pS[:, tb * 4:(tb + 1) * 4, :].rearrange("p a t -> p (a t)")
                    TEN.wait_ge(sV, nV[0])
                    for kh in range(2):
                        hk = 2 * gidx + kh
                        wait_half(hk)
                        for d4 in range(4):
                            db = kh * 4 + d4
                            TEN.matmul(bank[:, 0:512], xnF[:, db, tb * 128:(tb + 1) * 128],
                                       wr[:, half_slots(hk)[d4], ch * 512:(ch + 1) * 512],
                                       start=(db == 0), stop=(db == 7))
                    tinc()
                    evac(tb, ch, bank[:, 0:512])
            tw_done()

        def evac_q(tb, ch, ps):
            VEC.wait_ge(sT, nT[0])
            vinc(VEC.tensor_copy(R_s[:, tb, :, :].rearrange("p h j -> p (h j)")[:, ch * 512:(ch + 1) * 512], ps))

        def evac_v(tb, ch, ps):
            VEC.wait_ge(sT, nT[0])
            vinc(VEC.tensor_copy(vTM[:, tb, :, :].rearrange("p h j -> p (h j)")[:, ch * 512:(ch + 1) * 512], ps))

        proj_group(0, evac_q)
        for tb in range(NTB):
            VEC.tensor_tensor(out=scr2[:, 0:D].rearrange("p (h j) -> p h j", h=H),
                              in0=R_s[:, tb, :, :], in1=R_s[:, tb, :, :], op=ALU.mult)
            VEC.drain()
            VEC.tensor_reduce(out=qss[:].unsqueeze(2),
                              in_=scr2[:, 0:D].rearrange("p (h j) -> p h j", h=H),
                              axis=mybir.AxisListType.X, op=ALU.add)
            VEC.drain()
            VEC.tensor_scalar(out=qss[:], in0=qss[:], scalar1=1e-6, scalar2=None, op0=ALU.add)
            vinc()
            SCA.wait_ge(sV, nV[0])
            scinc(SCA.activation(out=t16a[:], in_=qss[:], func=AF.Sqrt))
            VEC.wait_ge(sSc, nSc[0])
            VEC.reciprocal(out=t16b[:], in_=t16a[:])
            VEC.drain()
            VEC.tensor_tensor(out=R_s[:, tb, :, :], in0=R_s[:, tb, :, :],
                              in1=t16b[:].unsqueeze(2).broadcast_to([128, H, DH]), op=ALU.mult)
            VEC.tensor_copy(P_s[:, tb, :, :].rearrange("p h j -> p (h j)"),
                            R_s[:, tb, :, :].rearrange("p h j -> p (h j)"))
            VEC.drain()
            VEC.tensor_tensor(out=scr2[:, 0:D].rearrange("p (h j) -> p h j", h=H),
                              in0=R_s[:, tb, :, :], in1=R_s[:, tb, :, :], op=ALU.mult)
            VEC.drain()
            VEC.tensor_reduce(out=rsA[:, tb, :].unsqueeze(2),
                              in_=scr2[:, 0:D].rearrange("p (h j) -> p h j", h=H),
                              axis=mybir.AxisListType.X, op=ALU.add)
            vinc(VEC.memset(X_s[:, tb, :, :].rearrange("p h j -> p (h j)"), 0.0))

        for tb in range(NTB):
            bank = pS[:, tb * 4:(tb + 1) * 4, :].rearrange("p a t -> p (a t)")
            for ch in range(2):
                TEN.wait_ge(sV, nV[0])
                for kh in range(2):
                    hk = 2 + kh
                    wait_half(hk)
                    for d4 in range(4):
                        db = kh * 4 + d4
                        TEN.matmul(bank[:, 0:512], xnF[:, db, tb * 128:(tb + 1) * 128],
                                   wr[:, half_slots(hk)[d4], ch * 512:(ch + 1) * 512],
                                   start=(db == 0), stop=(db == 7))
                tinc()
                VEC.wait_ge(sT, nT[0])
                vinc(VEC.tensor_copy(scr[:, ch * 512:(ch + 1) * 512], bank[:, 0:512]))
            VEC.tensor_tensor(out=scr2[:, 0:D].rearrange("p (h j) -> p h j", h=H),
                              in0=scr[:, 0:D].rearrange("p (h j) -> p h j", h=H),
                              in1=scr[:, 0:D].rearrange("p (h j) -> p h j", h=H), op=ALU.mult)
            VEC.drain()
            VEC.tensor_reduce(out=qss[:].unsqueeze(2),
                              in_=scr2[:, 0:D].rearrange("p (h j) -> p h j", h=H),
                              axis=mybir.AxisListType.X, op=ALU.add)
            VEC.drain()
            VEC.tensor_scalar(out=qss[:], in0=qss[:], scalar1=1e-6, scalar2=None, op0=ALU.add)
            vinc()
            SCA.wait_ge(sV, nV[0])
            scinc(SCA.activation(out=t16a[:], in_=qss[:], func=AF.Sqrt))
            VEC.wait_ge(sSc, nSc[0])
            VEC.reciprocal(out=t16b[:], in_=t16a[:])
            VEC.drain()
            VEC.tensor_tensor(out=kTM[:, tb, :, :],
                              in0=scr[:, 0:D].rearrange("p (h j) -> p h j", h=H),
                              in1=t16b[:].unsqueeze(2).broadcast_to([128, H, DH]), op=ALU.mult)
            vinc()
        tw_done()

        proj_group(2, evac_v)

        if stage < 3:
            continue
        for tb in range(NTB):
            TEN.wait_ge(sV, nV[0]); TEN.wait_ge(sSc, nSc[0])
            for db in range(8):
                TEN.matmul(pAp_f[:, 0:40], xnF[:, db, tb * 128:(tb + 1) * 128],
                           fbw_s[:, db, :], start=(db == 0), stop=(db == 7))
            tinc()
            VEC.wait_ge(sT, nT[0])
            VEC.tensor_tensor(out=scr[:, 0:H], in0=pAp_f[:, 0:H], in1=delta_s[:], op=ALU.add)
            vinc()
            SCA.wait_ge(sV, nV[0])
            SCA.activation(out=scr[:, H:2 * H].rearrange("p a -> p a"), in_=scr[:, 0:H], func=AF.Abs)
            SCA.drain()
            scinc(SCA.activation(out=scr[:, H:2 * H], in_=scr[:, H:2 * H], func=AF.Exp, scale=-1.0))
            VEC.wait_ge(sSc, nSc[0])
            VEC.tensor_scalar(out=scr[:, H:2 * H], in0=scr[:, H:2 * H], scalar1=1.0, scalar2=None, op0=ALU.add)
            vinc()
            SCA.wait_ge(sV, nV[0])
            scinc(SCA.activation(out=scr[:, H:2 * H], in_=scr[:, H:2 * H], func=AF.Ln))
            scinc(SCA.activation(out=beta_t[:, tb, :], in_=pAp_f[:, H:2 * H], func=AF.Sigmoid))
            VEC.wait_ge(sSc, nSc[0])
            VEC.tensor_single_scalar(out=scr[:, 2 * H:3 * H], in_=scr[:, 0:H], scalar=0.0, op=ALU.min)
            VEC.drain()
            VEC.tensor_tensor(out=fln[:, tb, :], in0=scr[:, 2 * H:3 * H], in1=scr[:, H:2 * H], op=ALU.subtract)
            VEC.drain()
            VEC.tensor_copy(fhl[:, tb, 0:16], fln[:, tb, :])
            VEC.drain()
            VEC.tensor_tensor(out=fhl[:, tb, 16:32], in0=fln[:, tb, :], in1=fhl[:, tb, 0:16], op=ALU.subtract)
            vinc()
            SCA.wait_ge(sV, nV[0])
            scinc(SCA.activation(out=lnb_t[:, tb, :], in_=beta_t[:, tb, :], func=AF.Ln))
            VEC.wait_ge(sSc, nSc[0])
            VEC.tensor_tensor(out=scr[:, 32:40], in0=pAp_f[:, 2 * H:2 * H + 8], in1=gbb[:], op=ALU.add)
            VEC.drain()
            VEC.reduce_max(out=qss[:, 0:1], in_=scr[:, 32:40], axis=mybir.AxisListType.X)
            VEC.drain()
            VEC.tensor_tensor(out=scr[:, 0:8], in0=scr[:, 32:40],
                              in1=qss[:, 0:1].broadcast_to([128, 8]), op=ALU.subtract)
            vinc()
            SCA.wait_ge(sV, nV[0])
            scinc(SCA.activation(out=scr[:, 8:16], in_=scr[:, 0:8], func=AF.Exp))
            VEC.wait_ge(sSc, nSc[0])
            VEC.tensor_reduce(out=qss[:, 1:2], in_=scr[:, 8:16], axis=mybir.AxisListType.X, op=ALU.add)
            VEC.drain()
            VEC.reciprocal(out=qss[:, 2:3], in_=qss[:, 1:2])
            VEC.drain()
            VEC.tensor_tensor(out=probs[:, tb, :], in0=scr[:, 8:16],
                              in1=qss[:, 2:3].broadcast_to([128, 8]), op=ALU.mult)
            VEC.drain()
            VEC.tensor_single_scalar(out=scr[:, 16:24], in_=probs[:, tb, :], scalar=0.1, op=ALU.is_ge)
            VEC.drain()
            VEC.tensor_tensor(out=scr[:, 24:32], in0=probs[:, tb, :], in1=scr[:, 16:24], op=ALU.mult)
            VEC.drain()
            VEC.tensor_reduce(out=qss[:, 3:4], in_=scr[:, 24:32], axis=mybir.AxisListType.X, op=ALU.add)
            VEC.drain()
            VEC.tensor_single_scalar(out=qss[:, 4:5], in_=qss[:, 3:4], scalar=0.0, op=ALU.is_equal)
            VEC.drain()
            VEC.tensor_tensor(out=qss[:, 5:6], in0=qss[:, 3:4], in1=qss[:, 4:5], op=ALU.add)
            VEC.drain()
            VEC.reciprocal(out=qss[:, 6:7], in_=qss[:, 5:6])
            VEC.drain()
            vinc(VEC.tensor_tensor(out=wfin[:, tb, :], in0=scr[:, 24:32],
                                   in1=qss[:, 6:7].broadcast_to([128, 8]), op=ALU.mult))

        TEN.wait_ge(sV, nV[0])
        for db in range(8):
            TEN.matmul(pAp_f[0:64, 0:512], g1_s[:, db, :], xnF[:, db, :],
                       start=(db == 0), stop=(db == 7))
        tinc()
        VEC.wait_ge(sT, nT[0])
        vinc(VEC.tensor_copy(t1F[:], pAp_f[0:64, 0:512]))
        for tb in range(NTB):
            TEN.wait_ge(sV, nV[0]); TEN.wait_ge(sSc, nSc[0])
            for chh in range(2):
                TEN.matmul(pS_f[:, chh * 512:(chh + 1) * 512], t1F[:, tb * 128:(tb + 1) * 128],
                           g2_s[:, chh * 512:(chh + 1) * 512], start=True, stop=True)
            tinc()
            SCA.wait_ge(sT, nT[0])
            scinc(SCA.activation(out=gate_s[:, tb, :, :].rearrange("p h j -> p (h j)"),
                                 in_=pS_f[:, 0:1024], func=AF.Sigmoid))

        if stage < 4:
            continue
        kF = u8k[:].rearrange("p (c hp t) -> p c hp t", c=NTB, hp=8)
        for cch in range(NTB):
            TEN.wait_ge(sV, nV[0]); TEN.wait_ge(sSc, nSc[0])
            TEN.matmul(pAp_f[:, 0:32], trim[:], fhl[:, cch, :], start=True, stop=True)
            TEN.matmul(pAp_f[:, 32:64], onesall[:], fhl[:, cch, :], start=True, stop=True)
            tinc()
            VEC.wait_ge(sT, nT[0])
            VEC.tensor_copy(scr[:, 0:64], pAp_f[:, 0:64])
            VEC.drain()
            VEC.tensor_tensor(out=Gc[:, cch, :], in0=scr[:, 0:16], in1=scr[:, 16:32], op=ALU.add)
            VEC.tensor_tensor(out=t16a[:], in0=scr[:, 32:48], in1=scr[:, 48:64], op=ALU.add)
            VEC.drain()
            VEC.tensor_tensor(out=nGsh[:, cch, :], in0=lnb_t[:, cch, :], in1=Gc[:, cch, :], op=ALU.subtract)
            VEC.drain()
            VEC.tensor_tensor(out=t16b[:], in0=t16a[:], in1=nGsh[:, cch, :], op=ALU.add)
            VEC.tensor_copy(ghl[:, 0:16], Gc[:, cch, :])
            VEC.drain()
            VEC.tensor_tensor(out=ghl[:, 16:32], in0=Gc[:, cch, :], in1=ghl[:, 0:16], op=ALU.subtract)
            vinc()
            SCA.wait_ge(sV, nV[0])
            scinc(SCA.activation(out=dec_s[:, cch, :], in_=Gc[:, cch, :], func=AF.Exp))
            scinc(SCA.activation(out=dece[:], in_=t16a[:], func=AF.Exp))
            scinc(SCA.activation(out=wpr[:], in_=t16b[:], func=AF.Exp))
            TEN.wait_ge(sV, nV[0])
            TEN.transpose(pPt[0:32, cch % 2, 0, :], ghl[:], ident[:])
            tinc()
            VEC.wait_ge(sT, nT[0])
            vinc(VEC.tensor_copy(GFs[:], pPt[0:32, cch % 2, 0, :]))
            SYN.wait_ge(sV, nV[0])
            for hh in range(32):
                dmas(grow[0:1, hh * 128:(hh + 1) * 128], GFs[hh:hh + 1, :])
            TEN.wait_ge(dmaS, 16 * nS[0])
            VEC.wait_ge(sSc, nSc[0])
            dsv = dec_s[:, cch, :].rearrange("p (a b) -> p a b", b=2)
            VEC.tensor_copy(dec_sh[:, cch, 0:8], dsv[:, :, 0:1].rearrange("p a b -> p (a b)"))
            VEC.tensor_copy(dec_sh[:, cch, 8:16], dsv[:, :, 1:2].rearrange("p a b -> p (a b)"))
            dev_ = dece[:].rearrange("p (a b) -> p a b", b=2)
            VEC.tensor_copy(dech[0:64, :], dev_[0:64, :, 0:1].rearrange("p a b -> p (a b)"))
            VEC.tensor_copy(dech[64:128, :], dev_[64:128, :, 1:2].rearrange("p a b -> p (a b)"))
            vinc(VEC.tensor_tensor(out=wKt[:], in0=kTM[:, cch, :, :],
                                   in1=wpr[:].unsqueeze(2).broadcast_to([128, H, DH]), op=ALU.mult))
            VEC.tensor_copy(Hb[:, cch, :, :].rearrange("p a b -> p (a b)"),
                            Hm[:, 0, :, :].rearrange("p a b -> p (a b)"))
            vinc(VEC.tensor_copy(Hbv[:, cch, :, :].rearrange("p a b -> p (a b)"),
                                 Hm[:, 1, :, :].rearrange("p a b -> p (a b)")))
            TEN.wait_ge(sV, nV[0])
            for h in HORD:
                pb = (h % 2) * 64
                TEN.matmul(pS[pb:pb + 64, h, 0:DH], wKt[:, h, :], kTM[:, cch, h, :],
                           start=True, stop=True, tile_position=(0, pb))
            for h in HORD:
                pb = (h % 2) * 64
                TEN.matmul(pAp[pb:pb + 64, h, :], wKt[:, h, :], vTM[:, cch, h, :],
                           start=True, stop=True, tile_position=(0, pb))
            tinc()
            VEC.wait_ge(sT, nT[0])
            VEC.tensor_tensor(out=Hm[:, 0, :, :], in0=Hm[:, 0, :, :],
                              in1=dech[:].unsqueeze(2).broadcast_to([128, 8, DH]), op=ALU.mult)
            VEC.tensor_tensor(out=Hm[:, 1, :, :], in0=Hm[:, 1, :, :],
                              in1=dech[:].unsqueeze(2).broadcast_to([128, 8, DH]), op=ALU.mult)
            VEC.drain()
            pSe = pS[:, :, 0:DH].rearrange("p (a b) j -> p a b j", b=2)
            pAe = pAp[:].rearrange("p (a b) j -> p a b j", b=2)
            VEC.tensor_tensor(out=Hm[0:64, 0, :, :], in0=Hm[0:64, 0, :, :],
                              in1=pSe[0:64, :, 0, :], op=ALU.add)
            VEC.tensor_tensor(out=Hm[64:128, 0, :, :], in0=Hm[64:128, 0, :, :],
                              in1=pSe[64:128, :, 1, :], op=ALU.add)
            VEC.tensor_tensor(out=Hm[0:64, 1, :, :], in0=Hm[0:64, 1, :, :],
                              in1=pAe[0:64, :, 0, :], op=ALU.add)
            VEC.tensor_tensor(out=Hm[64:128, 1, :, :], in0=Hm[64:128, 1, :, :],
                              in1=pAe[64:128, :, 1, :], op=ALU.add)
            vinc()
            TEN.wait_ge(sV, nV[0])
            for h in HORD:
                pb = (h % 2) * 64
                TEN.transpose(pPt[pb:pb + 64, cch % 2, h // 2, :], kTM[:, cch, h, :], ident[:],
                              tile_position=(0, pb))
            tinc()
            VEC.wait_ge(sT, nT[0])
            vinc(VEC.tensor_copy(kF[:, cch, :, :].rearrange("p a b -> p (a b)"),
                                 pPt[:, cch % 2, :, :].rearrange("p a b -> p (a b)")))
            TEN.wait_ge(sV, nV[0])
            for h in range(H):
                TEN.matmul(pS[:, h, :], onesb[:], grow[0:1, h * 128:(h + 1) * 128], start=True, stop=False)
                TEN.matmul(pS[:, h, :], onesb[:], grow[0:1, 2048 + h * 128:2048 + (h + 1) * 128], start=False, stop=True)
            tinc()
            VEC.wait_ge(sT, nT[0])
            for h in range(H):
                dst = (scr if h < 8 else scr2)[:, (h % 8) * 128:(h % 8 + 1) * 128]
                VEC.tensor_tensor(out=dst, in0=pS[:, h, :],
                                  in1=nGsh[:, cch, h:h + 1].broadcast_to([128, 128]), op=ALU.add)
                VEC.tensor_tensor(out=dst, in0=dst,
                                  in1=lnb_t[:, cch, h:h + 1].broadcast_to([128, 128]), op=ALU.min)
            vinc()
            SCA.wait_ge(sV, nV[0])
            for h in range(H):
                buf_ = (scr if h < 8 else scr2)[:, (h % 8) * 128:(h % 8 + 1) * 128]
                SCA.activation(out=buf_, in_=buf_, func=AF.Exp)
            scinc()
            VEC.wait_ge(sSc, nSc[0])
            for h in range(H):
                buf_ = (scr if h < 8 else scr2)[:, (h % 8) * 128:(h % 8 + 1) * 128]
                VEC.tensor_tensor(out=M_s[:, cch, HORD.index(h), :], in0=buf_, in1=trim[:], op=ALU.mult)
            vinc()

        barrier()
        VEC.sem_clear(cT1); VEC.sem_clear(cT2); VEC.sem_clear(cT3)
        VEC.sem_clear(cPt); VEC.sem_clear(cSh); VEC.sem_clear(cAp); VEC.sem_clear(cP)
        VEC.sem_inc(cP, 4)
        VEC.sem_inc(cPt, 2)
        barrier()

        if stage < 5:
            continue
        import contextlib as _ctx
        _unroll = int(os.environ.get("K_UNROLL", 0))
        if not _unroll:
            TEN.reg_mov(rT1, 1); TEN.reg_mov(rT3, 3)
            TEN.reg_mov(rT0, 0); TEN.reg_mov(rTs, 1)
        with (_ctx.nullcontext(None) if _unroll else nc.tensor.Fori(0, cg_iters)) as it:
            for _it2 in (range(cg_iters) if _unroll else [it]):
              for cch in range(NTB):
                if _unroll:
                    step = _it2 * 4 + cch
                    w_p1, w_p3, w_p0 = step + 1, step + 3, step
                    w_sh = lambda g_, s_=step: s_ * 4 + g_ + 1
                else:
                    w_p1, w_p3, w_p0 = rT1, rT3, rT0
                    w_sh = lambda g_: rTs
                TEN.wait_ge(cP, w_p1)
                TEN.wait_ge(cPt, w_p1)
                for h in HORD:
                    pb = (h % 2) * 64
                    TEN.transpose(pPt[pb:pb + 64, cch % 2, h // 2, :], P_s[:, cch, h, :],
                                  ident[:], tile_position=(0, pb))
                _last["T"].then_inc(cT1, 1)
                if cgl >= 3:
                    TEN.wait_ge(cPt, w_p3)
                    if cgl >= 6:
                        TEN.wait_ge(cAp, w_p0)
                    for i, h in enumerate(HORD):
                        pb = (h % 2) * 64
                        TEN.matmul(pS[:, i, :], kF[pb:pb + 64, cch, h // 2, :],
                                   Ptf[pb:pb + 64, cch % 2, h // 2, :],
                                   start=True, stop=True, tile_position=(pb, 0))
                        if i % 4 == 3:
                            _last["T"].then_inc(cT2, 1)
                if cgl >= 5:
                    for i, h in enumerate(HORD):
                        if i % 4 == 0:
                            TEN.wait_ge(cSh, w_sh(i // 4))
                            if not _unroll:
                                TEN.reg_add(rTs, rTs, 1)
                        TEN.matmul(pAp[:, h, :], Sh_s[:, cch % 2, i, :], kTM[:, cch, h, :],
                                   start=True, stop=True)
                    for i, h in enumerate(HORD):
                        pb = (h % 2) * 64
                        TEN.matmul(pS[:, i, 0:DH], Ptf[pb:pb + 64, cch % 2, h // 2, :],
                                   Hb[pb:pb + 64, cch, h // 2, :],
                                   start=True, stop=True, tile_position=(pb, 0))
                    _last["T"].then_inc(cT3, 1)
                if not _unroll:
                    TEN.reg_add(rT1, rT1, 1); TEN.reg_add(rT3, rT3, 1)
                    TEN.reg_add(rT0, rT0, 1)

        if not _unroll:
            SCA.reg_mov(rS1, 1)
        with (_ctx.nullcontext(None) if _unroll else nc.scalar.Fori(0, cg_iters)) as it:
            for _it2 in (range(cg_iters) if _unroll else [it]):
              for cch in range(NTB):
                s_p1 = (_it2 * 4 + cch + 1) if _unroll else rS1
                SCA.wait_ge(cT1, s_p1)
                SCA.activation(out=Ptf[:, cch % 2, :, :].rearrange("p a b -> p (a b)"),
                               in_=pPt[:, cch % 2, :, :].rearrange("p a b -> p (a b)"),
                               func=AF.Copy)
                _last["S"].then_inc(cPt, 1)
                if not _unroll:
                    SCA.reg_add(rS1, rS1, 1)

        if not _unroll:
            VEC.reg_mov(rV1, 1); VEC.reg_mov(rVs, 1)
        with (_ctx.nullcontext(None) if _unroll else nc.vector.Fori(0, cg_iters)) as it:
            for _it2 in (range(cg_iters) if _unroll else [it]):
              for cch in range(NTB):
                if _unroll:
                    step = _it2 * 4 + cch
                    v_p1 = step + 1
                    v_sh = lambda g_, s_=step: s_ * 4 + g_ + 1
                else:
                    v_p1 = rV1
                    v_sh = lambda g_: rVs

                if cgl < 4:
                    if not _unroll:
                        VEC.reg_add(rV1, rV1, 1)
                    VEC.memset(junkb[0:1, 0:1], 0.0).then_inc(cP, 1)
                    continue
                for g in range(4):
                    VEC.wait_ge(cT2, v_sh(g))
                    if not _unroll:
                        VEC.reg_add(rVs, rVs, 1)
                    VEC.tensor_tensor(out=Sh_s[:, cch % 2, 4 * g:4 * g + 4, :],
                                      in0=pS[:, 4 * g:4 * g + 4, :],
                                      in1=M_s[:, cch, 4 * g:4 * g + 4, :], op=ALU.mult)
                    _last["V"].then_inc(cSh, 1)
                if cgl < 6:
                    if not _unroll:
                        VEC.reg_add(rV1, rV1, 1)
                    VEC.memset(junkb[0:1, 0:1], 0.0).then_inc(cP, 1)
                    continue
                VEC.wait_ge(cT3, v_p1)
                if not _unroll:
                    VEC.reg_add(rV1, rV1, 1)
                VEC.tensor_tensor(out=TlP[:].rearrange("p h j -> p (h j)"),
                                  in0=P_s[:, cch, :, :].rearrange("p h j -> p (h j)"),
                                  in1=lbc[:], op=ALU.mult)
                VEC.tensor_tensor(out=Ap_s[:, 0, :, :].rearrange("p h j -> p (h j)"),
                                  in0=pAp_f[:], in1=TlP[:].rearrange("p h j -> p (h j)"), op=ALU.add)
                VEC.tensor_tensor(out=scr2[:, 0:D].rearrange("p (h j) -> p h j", h=H),
                                  in0=pS[:, :, 0:DH],
                                  in1=dec_sh[:, cch, :].unsqueeze(2).broadcast_to([128, H, DH]),
                                  op=ALU.mult)
                Ap_eo = Ap_s[:, 0, :, :].rearrange("p (a b) j -> p a b j", b=2)
                VEC.tensor_tensor(out=Ap_eo[:, :, 0, :], in0=Ap_eo[:, :, 0, :],
                                  in1=scr2[:, 0:512].rearrange("p (a j) -> p a j", a=8), op=ALU.add)
                VEC.tensor_tensor(out=Ap_eo[:, :, 1, :], in0=Ap_eo[:, :, 1, :],
                                  in1=scr2[:, 512:1024].rearrange("p (a j) -> p a j", a=8), op=ALU.add)
                _last["V"].then_inc(cAp, 1)
                if cgl < 7:
                    VEC.memset(junkb[0:1, 0:1], 0.0).then_inc(cP, 1)
                    continue
                VEC.tensor_tensor(out=scr2[:, 0:D].rearrange("p (h j) -> p h j", h=H),
                                  in0=Ap_s[:, 0, :, :], in1=P_s[:, cch, :, :], op=ALU.mult)
                VEC.tensor_reduce(out=pap_t[:, cch, :].unsqueeze(2),
                                  in_=scr2[:, 0:D].rearrange("p (h j) -> p h j", h=H),
                                  axis=mybir.AxisListType.X, op=ALU.add)
                spacer()
                VEC.tensor_scalar(out=t16a[:], in0=pap_t[:, cch, :], scalar1=1e-12, scalar2=None, op0=ALU.add)
                VEC.tensor_scalar(out=t16c[:], in0=rsA[:, cch, :], scalar1=1e-12, scalar2=None, op0=ALU.add)
                spacer()
                VEC.reciprocal(out=t16b[:], in_=t16a[:])
                VEC.reciprocal(out=t16d[:], in_=t16c[:])
                spacer()
                VEC.tensor_tensor(out=alp_t[:], in0=rsA[:, cch, :], in1=t16b[:], op=ALU.mult)
                spacer()
                VEC.tensor_tensor(out=scr[:, 0:D].rearrange("p (h j) -> p h j", h=H), in0=P_s[:, cch, :, :],
                                  in1=alp_t[:].unsqueeze(2).broadcast_to([128, H, DH]), op=ALU.mult)
                VEC.tensor_tensor(out=X_s[:, cch, :, :], in0=X_s[:, cch, :, :],
                                  in1=scr[:, 0:D].rearrange("p (h j) -> p h j", h=H), op=ALU.add)
                VEC.tensor_tensor(out=Ap_s[:, 0, :, :], in0=Ap_s[:, 0, :, :],
                                  in1=alp_t[:].unsqueeze(2).broadcast_to([128, H, DH]), op=ALU.mult)
                VEC.tensor_tensor(out=R_s[:, cch, :, :], in0=R_s[:, cch, :, :],
                                  in1=Ap_s[:, 0, :, :], op=ALU.subtract)
                VEC.tensor_tensor(out=scr2[:, 0:D].rearrange("p (h j) -> p h j", h=H),
                                  in0=R_s[:, cch, :, :], in1=R_s[:, cch, :, :], op=ALU.mult)
                VEC.tensor_reduce(out=rsB[:, cch, :].unsqueeze(2),
                                  in_=scr2[:, 0:D].rearrange("p (h j) -> p h j", h=H),
                                  axis=mybir.AxisListType.X, op=ALU.add)
                spacer()
                VEC.tensor_tensor(out=bet_t[:], in0=rsB[:, cch, :], in1=t16d[:], op=ALU.mult)
                VEC.tensor_copy(rsA[:, cch, :], rsB[:, cch, :])
                spacer()
                VEC.tensor_tensor(out=scr[:, 0:D].rearrange("p (h j) -> p h j", h=H), in0=P_s[:, cch, :, :],
                                  in1=bet_t[:].unsqueeze(2).broadcast_to([128, H, DH]), op=ALU.mult)
                VEC.tensor_tensor(out=P_s[:, cch, :, :], in0=scr[:, 0:D].rearrange("p (h j) -> p h j", h=H),
                                  in1=R_s[:, cch, :, :], op=ALU.add)
                _last["V"].then_inc(cP, 1)

        barrier()

        if stage < 6:
            continue
        for cch in range(NTB):
            VEC.tensor_copy(P_s[:, cch, :, :].rearrange("p h j -> p (h j)"),
                            X_s[:, cch, :, :].rearrange("p h j -> p (h j)"))
            vinc()
            TEN.wait_ge(sV, nV[0])
            for h in HORD:
                pb = (h % 2) * 64
                TEN.transpose(pPt[pb:pb + 64, cch % 2, h // 2, :], P_s[:, cch, h, :],
                              ident[:], tile_position=(0, pb))
            tinc()
            VEC.wait_ge(sT, nT[0])
            vinc(VEC.tensor_copy(Ptf[:, cch % 2, :, :].rearrange("p a b -> p (a b)"),
                                 pPt[:, cch % 2, :, :].rearrange("p a b -> p (a b)")))
            TEN.wait_ge(sV, nV[0])
            for i, h in enumerate(HORD):
                pb = (h % 2) * 64
                TEN.matmul(pS[:, i, :], kF[pb:pb + 64, cch, h // 2, :],
                           Ptf[pb:pb + 64, cch % 2, h // 2, :],
                           start=True, stop=True, tile_position=(pb, 0))
            tinc()
            VEC.wait_ge(sT, nT[0])
            for g in range(4):
                VEC.tensor_tensor(out=Sh_s[:, cch % 2, 4 * g:4 * g + 4, :],
                                  in0=pS[:, 4 * g:4 * g + 4, :],
                                  in1=M_s[:, cch, 4 * g:4 * g + 4, :], op=ALU.mult)
            vinc()
            TEN.wait_ge(sV, nV[0])
            for i, h in enumerate(HORD):
                TEN.matmul(pAp[:, h, :], Sh_s[:, cch % 2, i, :], vTM[:, cch, h, :],
                           start=True, stop=True)
            for i, h in enumerate(HORD):
                pb = (h % 2) * 64
                TEN.matmul(pS[:, i, 0:DH], Ptf[pb:pb + 64, cch % 2, h // 2, :],
                           Hbv[pb:pb + 64, cch, h // 2, :],
                           start=True, stop=True, tile_position=(pb, 0))
            tinc()
            VEC.wait_ge(sT, nT[0])
            VEC.tensor_copy(Ap_s[:, 0, :, :].rearrange("p h j -> p (h j)"), pAp_f[:])
            for i, h in enumerate(HORD):
                VEC.scalar_tensor_tensor(out=Ap_s[:, 0, h, :], in0=pS[:, i, 0:DH],
                                         scalar=dec_s[:, cch, h:h + 1],
                                         in1=Ap_s[:, 0, h, :], op0=ALU.mult, op1=ALU.add)
            VEC.tensor_tensor(out=Ap_s[:, 0, :, :].rearrange("p h j -> p (h j)"),
                              in0=Ap_s[:, 0, :, :].rearrange("p h j -> p (h j)"),
                              in1=gate_s[:, cch, :, :].rearrange("p h j -> p (h j)"), op=ALU.mult)
            VEC.drain()
            VEC.tensor_tensor(out=scr2[:, 0:D].rearrange("p (h j) -> p h j", h=H),
                              in0=Ap_s[:, 0, :, :], in1=Ap_s[:, 0, :, :], op=ALU.mult)
            VEC.drain()
            VEC.tensor_reduce(out=qss[:].unsqueeze(2),
                              in_=scr2[:, 0:D].rearrange("p (h j) -> p h j", h=H),
                              axis=mybir.AxisListType.X, op=ALU.add)
            VEC.drain()
            VEC.tensor_scalar(out=qss[:], in0=qss[:], scalar1=1.0 / DH, scalar2=1e-5, op0=ALU.mult, op1=ALU.add)
            vinc()
            SCA.wait_ge(sV, nV[0])
            scinc(SCA.activation(out=t16a[:], in_=qss[:], func=AF.Sqrt))
            VEC.wait_ge(sSc, nSc[0])
            VEC.reciprocal(out=t16b[:], in_=t16a[:])
            VEC.drain()
            vinc(VEC.tensor_tensor(out=xnT[:, cch, :].rearrange("p (h j) -> p h j", h=H),
                                   in0=Ap_s[:, 0, :, :],
                                   in1=t16b[:].unsqueeze(2).broadcast_to([128, H, DH]), op=ALU.mult))

        if stage < 7:
            continue
        ogF = u8k[:].rearrange("p (db t) -> p db t", db=8)
        nT_base = nT[0]; nV_base = nV[0]
        for tb in range(NTB):
            for db in range(8):
                idx = tb * 8 + db
                TEN.wait_ge(sV, nV_base if idx < 2 else nV_base + idx - 1)
                tinc(TEN.transpose(pPt[:, idx % 2, 0, :], xnT[:, tb, db * 128:(db + 1) * 128],
                                   ident[:]))
        for tb in range(NTB):
            for db in range(8):
                idx = tb * 8 + db
                VEC.wait_ge(sT, nT_base + idx + 1)
                vinc(VEC.tensor_copy(ogF[:, db, tb * 128:(tb + 1) * 128], pPt[:, idx % 2, 0, :]))
        s_t = X_s[:].rearrange("p c h j -> p c (h j)")
        for ch in range(2):
            for tb in range(NTB):
                bank = pS[:, tb * 4:(tb + 1) * 4, :].rearrange("p a t -> p (a t)")
                TEN.wait_ge(sV, nV[0])
                for kh in range(2):
                    hk = 6 + kh
                    wait_half(hk)
                    for d4 in range(4):
                        db = kh * 4 + d4
                        TEN.matmul(bank[:, 0:512], ogF[:, db, tb * 128:(tb + 1) * 128],
                                   wr[:, half_slots(hk)[d4], ch * 512:(ch + 1) * 512],
                                   start=(db == 0), stop=(db == 7))
                tinc()
                VEC.wait_ge(sT, nT[0])
                vinc(VEC.tensor_copy(s_t[:, tb, ch * 512:(ch + 1) * 512], bank[:, 0:512]))
        tw_done()

        if stage < 8:
            continue
        acc = R_s[:].rearrange("p c h j -> p c (h j)")
        for e in range(8):
            for ch in range(2):
                for tb in range(NTB):
                    TEN.wait_ge(sV, nV[0])
                    for kh in range(2):
                        hk = 8 + 2 * e + kh
                        wait_half(hk)
                        for d4 in range(4):
                            db = kh * 4 + d4
                            TEN.matmul(pS_f[:, ch * 512:(ch + 1) * 512],
                                       xnF[:, db, tb * 128:(tb + 1) * 128],
                                       wr[:, half_slots(hk)[d4], ch * 512:(ch + 1) * 512],
                                       start=(db == 0), stop=(db == 7))
                    tinc()
                    VEC.wait_ge(sT, nT[0])
                    if e == 0:
                        vinc(VEC.tensor_scalar(out=acc[:, tb, ch * 512:(ch + 1) * 512],
                                               in0=pS_f[:, ch * 512:(ch + 1) * 512],
                                               scalar1=wfin[:, tb, e:e + 1], scalar2=None, op0=ALU.mult))
                    else:
                        vinc(VEC.scalar_tensor_tensor(out=acc[:, tb, ch * 512:(ch + 1) * 512],
                                                      in0=pS_f[:, ch * 512:(ch + 1) * 512],
                                                      scalar=wfin[:, tb, e:e + 1],
                                                      in1=acc[:, tb, ch * 512:(ch + 1) * 512],
                                                      op0=ALU.mult, op1=ALU.add))
            tw_done()

        for tb in range(NTB):
            VEC.drain()
            VEC.tensor_copy(wf8[:], wfin[:, tb, :])
            vinc()
            TEN.wait_ge(sV, nV[0])
            tinc(TEN.transpose(pPt[0:8, 0, 0, :], wf8[:], ident[:]))
            VEC.wait_ge(sT, nT[0])
            vinc(VEC.tensor_copy(wfT[:], pPt[0:8, 0, 0, :]))
            TEN.wait_ge(sV, nV[0])
            for ch in range(2):
                TEN.matmul(pAp_f[:, ch * 512:(ch + 1) * 512], wfT[:], eb_s[:, ch * 512:(ch + 1) * 512],
                           start=True, stop=True)
            tinc()
            VEC.wait_ge(sT, nT[0])
            VEC.tensor_tensor(out=scr[:, 0:D], in0=acc[:, tb, :], in1=pAp_f[:], op=ALU.add)
            vinc()
            SYN.wait_ge(sV, nV[0])
            dmas(x_s[:], x_in[t0 + tb * 128: t0 + (tb + 1) * 128, :])
            SCA.wait_ge(sV, nV[0])
            scinc(SCA.activation(out=scr2[:, 0:D], in_=scr[:, 0:D], func=AF.Silu))
            VEC.wait_ge(sSc, nSc[0])
            VEC.tensor_tensor(out=scr[:, 0:D], in0=s_t[:, tb, :], in1=scr2[:, 0:D], op=ALU.mult)
            VEC.wait_ge(dmaS, 16 * nS[0])
            vinc(VEC.tensor_tensor(out=scr2[:, 0:D], in0=x_s[:], in1=scr[:, 0:D], op=ALU.add))
            SYN.wait_ge(sV, nV[0])
            dmas(out_t[t0 + tb * 128: t0 + (tb + 1) * 128, :], scr2[:, 0:D])

        barrier()

    dump = os.environ.get("K_DUMP", "")
    if dump:
        src_map = dict(M_s=M_s, P_s=P_s, kTM=kTM, vTM=vTM, R_s=R_s, X_s=X_s,
                       xnT=xnT, gate_s=gate_s, Hm=Hm, dec_s=dec_s, Gc=Gc,
                       lbc=lbc, Hb=Hb, Hbv=Hbv, Sh_s=Sh_s, Ap_s=Ap_s, rsA=rsA, Ptf=Ptf, scr=scr, scr2=scr2, qss=qss, t1F=t1F,
                       u8k=u8k, beta_t=beta_t, wfin=wfin, xnF=xnF)
        t_ = src_map[dump]
        ap_ = t_[:]
        if ap_.ndim > 2:
            ap_ = ap_.rearrange("p " + " ".join(chr(97 + i) for i in range(ap_.ndim - 1))
                                + " -> p (" + " ".join(chr(97 + i) for i in range(ap_.ndim - 1)) + ")")
        n_ = min(ap_.shape[1], 8192)
        npart = ap_.shape[0]
        for off in range(0, n_, 1024):
            w_ = min(1024, n_ - off)
            VEC.tensor_copy(scr[0:npart, 0:w_], ap_[:, off:off + w_])
            vinc()
            SYN.wait_ge(sV, nV[0])
            dmas(dbg_t[0:npart, off:off + w_], scr[0:npart, 0:w_])
            SYN.wait_ge(dmaS, 16 * nS[0])
            VEC.wait_ge(dmaS, 16 * nS[0])
    SYN.wait_ge(dmaS, 16 * nS[0])
    barrier()
    es.close()
    return nc


_CONSTS = None


def _consts():
    global _CONSTS
    if _CONSTS is None:
        ident = np.eye(128, dtype=np.float32)
        triu = (np.arange(128)[:, None] <= np.arange(128)[None, :]).astype(np.float32)
        _CONSTS = dict(cid16=ident.astype(BF), cidf=ident, ctriu=triu,
                       ctrim=triu.astype(BF))
    return _CONSTS


def make_inputs(x, inp):
    shared = dict(
        wq=np.ascontiguousarray(inp["L0_attn_Wq"], np.float32),
        wk=np.ascontiguousarray(inp["L0_attn_Wk"], np.float32),
        wv=np.ascontiguousarray(inp["L0_attn_Wv"], np.float32),
        wf=np.ascontiguousarray(inp["L0_attn_Wf"], np.float32),
        wb=np.ascontiguousarray(inp["L0_attn_Wbet"], np.float32),
        wo=np.ascontiguousarray(inp["L0_attn_Wo"], np.float32),
        delta=np.ascontiguousarray(inp["L0_attn_delta"], np.float32).reshape(1, H),
        lambp=np.ascontiguousarray(inp["L0_attn_lamb_params"], np.float32).reshape(1, D),
        nw=np.ascontiguousarray(inp["L0_attn_norm_weight"], np.float32).reshape(1, D),
        g1=np.ascontiguousarray(inp["L0_attn_gate_w1"], np.float32),
        g2=np.ascontiguousarray(inp["L0_attn_gate_w2"], np.float32),
        gw=np.ascontiguousarray(inp["L0_ffn_gate_w"], np.float32),
        gb=np.ascontiguousarray(inp["L0_ffn_gate_b"], np.float32).reshape(1, 8),
        we=np.ascontiguousarray(inp["L0_ffn_expert_w"], np.float32),
        eb=np.ascontiguousarray(inp["L0_ffn_expert_b"], np.float32),
        **_consts(),
    )
    return [dict(x=np.ascontiguousarray(x[i]), **shared) for i in range(x.shape[0])]


def kernel(L0_attn_x, **kw):
    x = np.ascontiguousarray(np.asarray(L0_attn_x, dtype=np.float32))
    inp = dict(kw); inp["L0_attn_x"] = x
    b = x.shape[0]
    nc = build()
    in_maps = make_inputs(x, inp)
    res = run_bass_kernel_spmd(nc, in_maps, core_ids=list(range(b)))
    return np.stack([np.asarray(r["out"]) for r in res.results], axis=0)


if __name__ == "__main__":
    nc = build(512, 4)
    print("built ok")
